# version 17
# baseline (speedup 1.0000x reference)
import contextlib
from collections import defaultdict

import numpy as np

import concourse.bass as bass
import concourse.mybir as mybir
from concourse.bass_utils import run_bass_kernel_spmd

F32 = mybir.dt.float32
BF16 = mybir.dt.bfloat16
I32 = mybir.dt.int32
U16 = mybir.dt.uint16
AF = mybir.ActivationFunctionType
ALU = mybir.AluOpType

R = 8
B_, T_, D = 4, 4096, 2048
NH = 16
TOK = 2048
NDC = 16
INV_SQRT_DH = float(np.float32(1.0) / np.float32(np.sqrt(np.float32(128.0))))
EPS = 1e-5

SLOT_LT = [3, 2, 1, 0]
SLOT_KMAX = [32, 28, 16, 12]
SLOT_MSTART = [20, 16, 4, 0]
N_MASK = 12
QG0 = {0: [0, 512, 3072, 3584], 1: [1024, 1536, 2048, 2560]}


def chunk_map(j):
    blk = j // 8
    r = [0, 1, 1, 0][blk]
    tc = (j % 8) + (8 if blk >= 2 else 0)
    return r, tc


def kv_runs(kmax):
    runs, j = [], 0
    while j < kmax:
        r, tc = chunk_map(j)
        ln = 1
        while j + ln < kmax:
            r2, tc2 = chunk_map(j + ln)
            if r2 == r and tc2 == tc + ln:
                ln += 1
            else:
                break
        runs.append((j, r, tc, ln))
        j += ln
    return runs


def build_program(dbg=False):
    nc = bass.Bass("TRN2", target_bir_lowering=False)

    xo = nc.dram_tensor("xo", [TOK, D], F32, kind="ExternalInput").ap()
    xq_d = nc.dram_tensor("xq", [TOK, D], F32, kind="ExternalInput").ap()
    wq_d = nc.dram_tensor("wq", [D, D], F32, kind="ExternalInput").ap()
    wk_d = nc.dram_tensor("wk", [D, D], F32, kind="ExternalInput").ap()
    wv_d = nc.dram_tensor("wv", [D, D], F32, kind="ExternalInput").ap()
    wo_d = nc.dram_tensor("wo", [D, D], F32, kind="ExternalInput").ap()
    bq_d = nc.dram_tensor("bq_r", [128, NDC], F32, kind="ExternalInput").ap()
    bk_d = nc.dram_tensor("bk_r", [128, NDC], F32, kind="ExternalInput").ap()
    bv_d = nc.dram_tensor("bv_row", [1, D], F32, kind="ExternalInput").ap()
    bo_d = nc.dram_tensor("bo_row", [1, D], F32, kind="ExternalInput").ap()
    gwp = nc.dram_tensor("gwp", [2048, 1024], I32, kind="ExternalInput").ap()
    gws = nc.dram_tensor("gws", [2048, 64], F32, kind="ExternalInput").ap()
    uwp = nc.dram_tensor("uwp", [2048, 1024], I32, kind="ExternalInput").ap()
    uws = nc.dram_tensor("uws", [2048, 64], F32, kind="ExternalInput").ap()
    dwp = nc.dram_tensor("dwp", [2048, 1024], I32, kind="ExternalInput").ap()
    dws = nc.dram_tensor("dws", [2048, 64], F32, kind="ExternalInput").ap()
    mrow_d = nc.dram_tensor("maskrow", [128, 128], F32, kind="ExternalInput").ap()
    out_d = nc.dram_tensor("out", [TOK, D], F32, kind="ExternalOutput").ap()

    qT_d = nc.dram_tensor("qT_d", [128, NH * TOK], BF16).ap()
    attnT_d = nc.dram_tensor("attnT_d", [128, NH * TOK], BF16).ap()
    x1_d = nc.dram_tensor("x1_d", [TOK, D], F32).ap()
    cc2_in = nc.dram_tensor("cc2_in", [128, 2 * 32768], BF16)
    cc2_out = nc.dram_tensor("cc2_out", [256, 2 * 32768], BF16)
    cc1_in = [nc.dram_tensor(f"cc1_in{m}", [128, 32768], BF16) for m in range(3)]
    cc1_out = [nc.dram_tensor(f"cc1_out{m}", [128 * R, 32768], BF16, addr_space="Shared")
               for m in range(3)]
    if dbg:
        d_qT = nc.dram_tensor("d_qT", [128, NH * TOK], BF16, kind="ExternalOutput").ap()
        d_kv = nc.dram_tensor("d_kv", [256, 2 * 32768], BF16, kind="ExternalOutput").ap()
        d_att = nc.dram_tensor("d_att", [128, NH * TOK], BF16, kind="ExternalOutput").ap()
        d_x1 = nc.dram_tensor("d_x1", [TOK, D], F32, kind="ExternalOutput").ap()
        d_w = nc.dram_tensor("d_w", [1024, 2048], BF16, kind="ExternalOutput").ap()
        d_wd = nc.dram_tensor("d_wd", [1024, 2048], BF16, kind="ExternalOutput").ap()

    es = contextlib.ExitStack()
    S1 = es.enter_context(nc.sbuf_tensor("S1", [128, 32768], BF16))
    S2 = es.enter_context(nc.sbuf_tensor("S2", [128, 32768], BF16))
    S3 = es.enter_context(nc.sbuf_tensor("S3", [128, 32768], BF16))
    S4 = es.enter_context(nc.sbuf_tensor("S4", [128, 8096], BF16))
    pf = [es.enter_context(nc.psum_tensor(f"pf{i}", [128, 512], F32)) for i in range(6)]
    pb = [es.enter_context(nc.psum_tensor(f"pb{i}", [128, 1024], BF16)) for i in range(2)]

    names = ["SD", "SO", "GD", "CS", "TS", "VS", "SS", "PT", "VE", "GI"]
    sems = {n: es.enter_context(nc.semaphore(n)) for n in names}
    SD, SO, GD, CS, TS, VS, SS, PT, VE, GI = (sems[n] for n in names)
    cnt = {n: 0 for n in names}

    def bump(n, k=1):
        cnt[n] += k
        return cnt[n]

    def carve(slab, boff, p, n, dt):
        esz = 2 if dt in (BF16, U16) else 4
        c0 = boff // 2
        return slab[0:p, c0: c0 + (n * esz) // 2].bitcast(dt)

    stage = [carve(S4, 0, 128, 512, BF16), carve(S4, 1024, 128, 512, BF16)]
    bcast = carve(S4, 2048, 128, 2048, BF16)
    colq = carve(S4, 6144, 128, 512, F32)
    rowi = carve(S4, 8192, 128, 1, F32)
    ident = carve(S4, 8196, 128, 128, BF16)
    ones_col = carve(S4, 8452, 128, 1, BF16)
    mrow = carve(S4, 8456, 128, 128, F32)
    bq_sb = carve(S4, 8968, 128, NDC, F32)
    bk_sb = carve(S4, 9032, 128, NDC, F32)
    stats = carve(S4, 9096, 128, 24, F32)
    mv = carve(S4, 9192, 128, 2, F32)
    sstd = carve(S4, 9200, 128, 1, F32)
    rstd = carve(S4, 9204, 128, 1, F32)
    ones_row = carve(S4, 9208, 1, 128, F32)
    ones_rowb = carve(S4, 9720, 1, 128, BF16)
    inv_sb = carve(S4, 9976, 1, 512, F32)
    browb = carve(S4, 12024, 1, 2048, BF16)
    eps_c = carve(S4, 16120, 128, 1, F32)

    xT = [carve(S2, 0, 128, 2048, F32), carve(S2, 8192, 128, 2048, F32)]
    xn_t = [carve(S2, 16384, 128, 2048, BF16), carve(S2, 20480, 128, 2048, BF16)]

    def att_c(b):
        base = b * 17408
        return (carve(S2, base, 128, 4096, BF16),
                carve(S2, base + 8192, 128, 4096, BF16),
                carve(S2, base + 16384, 128, 512, BF16))
    att_bufs = [att_c(0), att_c(1)]
    masks = carve(S2, 34816, 128, N_MASK * 512, BF16)
    evT = [carve(S2, 47104, 128, 512, BF16), carve(S2, 48128, 128, 512, BF16)]
    ptile = [carve(S2, 49152, 128, 512, BF16), carve(S2, 50176, 128, 512, BF16),
             carve(S2, 53248, 128, 512, BF16)]
    invb = carve(S2, 51200, 128, 512, F32)
    lntmp = carve(S2, 55296, 1, 512, F32)

    p5_at = [carve(S2, 0, 128, 2048, BF16), carve(S2, 4096, 128, 2048, BF16)]
    p5_y = carve(S2, 8192, 128, 2048, F32)
    p5_x = carve(S2, 16384, 128, 2048, F32)
    p5_xn = carve(S2, 24576, 128, 2048, BF16)

    acc = carve(S2, 0, 128, 4 * 2048, F32)
    wd_ring = [carve(S2, 32768 + i * 4096, 128, 2048, BF16) for i in range(8)]

    dec_t = [carve(S1, i * 4096, 128, 1024, I32) for i in range(6)]
    pk_t = [carve(S1, 24576, 128, 1024, I32), carve(S1, 28672, 128, 1024, I32)]
    decb = [carve(S1, 32768, 128, 2048, BF16), carve(S1, 36864, 128, 2048, BF16)]
    dsc = [carve(S1, 40960, 128, 64, F32), carve(S1, 41216, 128, 64, F32)]

    wg_t = [carve(S3, 0, 128, 2048, BF16), carve(S3, 4096, 128, 2048, BF16)]
    wu_t = [carve(S3, 8192, 128, 2048, BF16), carve(S3, 12288, 128, 2048, BF16)]
    it_ring = [carve(S3, 16384 + i * 1024, 128, 512, BF16) for i in range(16)]
    g_sq = carve(S3, 32768, 128, 512, F32)
    g_t2 = carve(S3, 34816, 128, 512, F32)
    g_th = carve(S3, 36864, 128, 512, F32)
    g_b = carve(S3, 38912, 128, 512, F32)
    p6_x1 = carve(S3, 40960, 128, 2048, F32)
    p6_o = carve(S3, 49152, 128, 2048, F32)

    NPH = 7
    streams = [defaultdict(list) for _ in range(NPH)]

    def add(ph, eng, fn):
        streams[ph][eng].append(fn)

    def wait(ph, eng, sem, val):
        if val and val > 0:
            add(ph, eng, lambda e, s=sem, v=val: e.wait_ge(s, v))

    ph = 0
    add(ph, "sync", lambda e: e.dma_start(out=bq_sb, in_=bq_d).then_inc(SD, 16))
    add(ph, "sync", lambda e: e.dma_start(out=bk_sb, in_=bk_d).then_inc(SD, 16))
    add(ph, "sync", lambda e: e.dma_start(out=mrow, in_=mrow_d).then_inc(SD, 16))
    sd_p0 = bump("SD", 48)
    add(ph, "gpsimd", lambda e: e.dma_start(out=browb, in_=bv_d).then_inc(GD, 16))
    gd_bv = bump("GD", 16)
    add(ph, "gpsimd", lambda e: e.iota(colq, pattern=[[1, 512]], base=0,
                                       channel_multiplier=0,
                                       allow_small_or_imprecise_dtypes=True))
    add(ph, "gpsimd", lambda e: e.iota(rowi, pattern=[[0, 1]], base=0,
                                       channel_multiplier=1,
                                       allow_small_or_imprecise_dtypes=True).then_inc(GI, 1))
    bump("GI")
    wait(ph, "dve", GI, cnt["GI"])
    add(ph, "dve", lambda e: e.tensor_scalar(ident, colq[:, 0:128], rowi, None, ALU.is_equal))
    add(ph, "dve", lambda e: e.memset(ones_col, 1.0))
    add(ph, "dve", lambda e: e.memset(eps_c, 1e-5))
    add(ph, "dve", lambda e: e.memset(ones_rowb, 1.0))
    add(ph, "dve", lambda e: e.memset(ones_row, 1.0).then_inc(VS, 1))
    bump("VS")
    vs_ones = cnt["VS"]
    p0_cp = {}
    for i in range(4):
        wait(ph, "pe", GD, gd_bv)
        wait(ph, "pe", VS, vs_ones)
        if i >= 2:
            wait(ph, "pe", VS, p0_cp[i - 2])
        add(ph, "pe", lambda e, i=i: e.matmul(
            pf[i % 2][:, :], ones_rowb, browb[0:1, i * 512:(i + 1) * 512],
            start=True, stop=True).then_inc(TS, 1))
        bump("TS")
        wait(ph, "dve", TS, cnt["TS"])
        add(ph, "dve", lambda e, i=i: e.tensor_copy(
            bcast[:, i * 512:(i + 1) * 512], pf[i % 2][:, :]).then_inc(VS, 1))
        p0_cp[i] = bump("VS")
    pf_hist = {0: cnt["VS"], 1: cnt["VS"], 2: 0, 3: 0, 4: 0, 5: 0}

    def ln_stats(ph, src_f32):
        v0 = cnt["VS"]
        def f(e, s=src_f32, v0=v0):
            for i in range(4):
                inst = e.bn_stats(stats[:, i * 6:(i + 1) * 6], s[:, i * 512:(i + 1) * 512])
            inst.then_inc(VS, 1)
            e.wait_ge(VS, v0 + 1)
            e.bn_aggr(mv, stats).then_inc(VS, 1)
        add(ph, "dve", f)
        bump("VS", 2)
        wait(ph, "act", VS, cnt["VS"])
        add(ph, "act", lambda e: e.activation(sstd, mv[:, 1:2], AF.Sqrt,
                                              bias=eps_c).then_inc(SS, 1))
        bump("SS")
        wait(ph, "dve", SS, cnt["SS"])
        v1 = cnt["VS"]
        add(ph, "dve", lambda e, v1=v1: (e.reciprocal(rstd, sstd).then_inc(VS, 1),
                                         e.wait_ge(VS, v1 + 1)))
        bump("VS")

    pb_free = [0, 0]

    def transposes(ph, src_bf16, dst_slab, tt):
        for half in range(2):
            bi = half
            wait(ph, "pe", SS, pb_free[bi])

            def tp(e, src=src_bf16, bank=pb[bi], half=half):
                for i in range(8):
                    dc = half * 8 + i
                    inst = e.transpose(bank[:, i * 128:(i + 1) * 128],
                                       src[:, dc * 128:(dc + 1) * 128], ident)
                inst.then_inc(TS, 1)
            add(ph, "pe", tp)
            bump("TS")
            wait(ph, "act", TS, cnt["TS"])

            def ev(e, bank=pb[bi], half=half, dst=dst_slab, tt=tt):
                for i in range(8):
                    dc = half * 8 + i
                    inst = e.copy(dst[:, dc * 2048 + tt * 128: dc * 2048 + tt * 128 + 128],
                                  bank[:, i * 128:(i + 1) * 128])
                inst.then_inc(SS, 1)
            add(ph, "act", ev)
            pb_free[bi] = bump("SS")
        return cnt["TS"]

    ph = 1
    add(ph, "gpsimd", lambda e: e.dma_start(
        out=S3[:, :].rearrange("p (c h) -> p c h", c=16), in_=wk_d.rearrange("(c p) h -> p c h", p=128)).then_inc(GD, 16))
    gd_wk = bump("GD", 16)

    def ln_pass(ph, src_dram):
        xfree = [0] * 16
        src_done = [0] * 16
        for tt in range(16):
            b = tt % 2
            if tt >= 2:
                wait(ph, "sync", VS, xfree[tt - 2])
            add(ph, "sync", lambda e, b=b, tt=tt, s=src_dram: e.dma_start(
                out=xT[b], in_=s[tt * 128:(tt + 1) * 128, :]).then_inc(SD, 16))
            bump("SD", 16)
            wait(ph, "dve", SD, cnt["SD"])
            ln_stats(ph, xT[b])
            if tt >= 2:
                wait(ph, "dve", TS, src_done[tt - 2])

            v0 = cnt["VS"]
            def app(e, b=b, v0=v0):
                e.tensor_scalar(xT[b], xT[b], mv[:, 0:1], None,
                                ALU.subtract).then_inc(VS, 1)
                e.wait_ge(VS, v0 + 1)
                e.tensor_scalar(xn_t[b], xT[b], rstd, None, ALU.mult).then_inc(VS, 1)
            add(ph, "dve", app)
            bump("VS")
            xfree[tt] = bump("VS")
            wait(ph, "pe", VS, cnt["VS"])
            src_done[tt] = transposes(ph, xn_t[b], S1, tt)

    ln_pass(ph, xo)

    ph = 2
    stage_dma = [0, 0]
    stage_use = [0]

    def proj_pass(ph, wslab, bias_sb, scale, dst_ap, dst_off, tokmajor, pfb, gd_need,
                  vhead=False):
        evac = {}
        wait(ph, "dve", SD, sd_p0)
        wait(ph, "dve", GD, gd_bv)
        for a in range(16):
            for b4 in range(4):
                idx = a * 4 + b4
                bankk = pfb + idx % 2
                bank = pf[bankk]
                if idx < 2:
                    wait(ph, "pe", GD, gd_need)
                    wait(ph, "pe", VS, pf_hist[bankk])
                else:
                    wait(ph, "pe", VS, evac[idx - 2])
                if tokmajor:
                    def mm(e, a=a, b4=b4, bank=bank, ws=wslab):
                        for dc in range(NDC):
                            inst = e.matmul(
                                bank[:, :],
                                S1[:, dc * 2048 + a * 128: dc * 2048 + (a + 1) * 128],
                                ws[:, dc * 2048 + b4 * 512: dc * 2048 + (b4 + 1) * 512],
                                start=(dc == 0), stop=(dc == NDC - 1))
                        inst.then_inc(TS, 1)
                else:
                    def mm(e, a=a, b4=b4, bank=bank, ws=wslab):
                        for dc in range(NDC):
                            inst = e.matmul(
                                bank[:, :],
                                ws[:, dc * 2048 + a * 128: dc * 2048 + (a + 1) * 128],
                                S1[:, dc * 2048 + b4 * 512: dc * 2048 + (b4 + 1) * 512],
                                start=(dc == 0), stop=(dc == NDC - 1))
                        inst.then_inc(TS, 1)
                add(ph, "pe", mm)
                bump("TS")
                su = stage_use[0]
                sb = stage[su % 2]
                wait(ph, "dve", TS, cnt["TS"])
                wait(ph, "dve", SD, stage_dma[su % 2])
                if tokmajor:
                    add(ph, "dve", lambda e, sb=sb, bank=bank, b4=b4: e.tensor_tensor(
                        sb, bank[:, :], bcast[:, b4 * 512:(b4 + 1) * 512],
                        ALU.add).then_inc(VS, 1))
                else:
                    add(ph, "dve", lambda e, sb=sb, bank=bank, a=a: e.tensor_scalar(
                        sb, bank[:, :], bias_sb[:, a:a + 1], None,
                        ALU.add).then_inc(VS, 1))
                evac[idx] = pf_hist[bankk] = bump("VS")
                wait(ph, "sync", VS, cnt["VS"])
                if vhead:
                    def vst(e, sb=sb, dst=dst_ap, a=a, b4=b4, dst_off=dst_off):
                        dv = dst[:, dst_off:dst_off + 32768].rearrange(
                            "p (h a2 n) -> p h a2 n", h=16, a2=16)
                        e.dma_start(out=dv[:, b4 * 4:(b4 + 1) * 4, a, :],
                                    in_=sb.rearrange("p (h n) -> p h n", h=4)
                                    ).then_inc(SD, 16)
                    add(ph, "sync", vst)
                else:
                    off = dst_off + a * 2048 + b4 * 512
                    add(ph, "sync", lambda e, sb=sb, dst=dst_ap, off=off: e.dma_start(
                        out=dst[:, off: off + 512], in_=sb).then_inc(SD, 16))
                stage_dma[su % 2] = bump("SD", 16)
                stage_use[0] += 1

    proj_pass(ph, S3, bk_sb, None, cc2_in.ap(), 0, False, 0, gd_wk)
    k_ts = cnt["TS"]
    add(ph, "gpsimd", lambda e: e.wait_ge(TS, k_ts))
    add(ph, "gpsimd", lambda e: e.dma_start(
        out=S3[:, :].rearrange("p (c h) -> p c h", c=16),
        in_=wv_d.rearrange("(c p) h -> p c h", p=128)).then_inc(GD, 16))
    gd_wv = bump("GD", 16)
    proj_pass(ph, S3, None, None, cc2_in.ap(), 32768, True, 2, gd_wv, vhead=True)
    v_ts = cnt["TS"]
    add(ph, "gpsimd", lambda e: e.wait_ge(TS, v_ts))
    add(ph, "gpsimd", lambda e: e.dma_start(
        out=S3[:, :].rearrange("p (c h) -> p c h", c=16),
        in_=wq_d.rearrange("(c p) h -> p c h", p=128)).then_inc(GD, 16))
    gd_wq = bump("GD", 16)

    wait(ph, "sync", TS, v_ts)
    ln_pass(ph, xq_d)

    xq_sd = cnt["SD"]
    wait(ph, "gpsimd", SD, xq_sd)
    wait(ph, "gpsimd", GD, gd_wq)
    add(ph, "gpsimd", lambda e: e.collective_compute(
        "AllGather", ALU.bypass,
        replica_groups=[[0, 1], [2, 3], [4, 5], [6, 7]],
        ins=[cc2_in.ap().opt()], outs=[cc2_out.ap().opt()]).then_inc(CS, 1))
    cs_kv = bump("CS")

    S2q = S2[:, :].rearrange("p (h bl rr a) -> p h bl rr a", h=16, bl=2, rr=64, a=16)
    q_evac = {}
    for a in range(16):
        for b4 in range(4):
            idx = a * 4 + b4
            bankk = idx % 2
            bank = pf[bankk]
            if idx < 2:
                wait(ph, "pe", GD, gd_wq)
                wait(ph, "pe", SS, cnt["SS"])
                wait(ph, "pe", VS, pf_hist[bankk])
            else:
                wait(ph, "pe", VS, q_evac[idx - 2])

            def mmq(e, a=a, b4=b4, bank=bank):
                for dc in range(NDC):
                    inst = e.matmul(
                        bank[:, :],
                        S3[:, dc * 2048 + a * 128: dc * 2048 + (a + 1) * 128],
                        S1[:, dc * 2048 + b4 * 512: dc * 2048 + (b4 + 1) * 512],
                        start=(dc == 0), stop=(dc == NDC - 1))
                inst.then_inc(TS, 1)
            add(ph, "pe", mmq)
            bump("TS")
            wait(ph, "dve", TS, cnt["TS"])

            def qev(e, a=a, b4=b4, bank=bank):
                dst = S2q[:, b4 * 4:(b4 + 1) * 4, :, :, a]
                srcv = bank[:, :].rearrange("p (h bl rr) -> p h bl rr", h=4, bl=2)
                e.tensor_scalar(dst, srcv, bq_sb[:, a:a + 1], INV_SQRT_DH,
                                ALU.add, ALU.mult).then_inc(VS, 1)
            add(ph, "dve", qev)
            q_evac[idx] = pf_hist[bankk] = bump("VS")
    wait(ph, "sync", VS, cnt["VS"])
    wait(ph, "sync", CS, cs_kv)
    add(ph, "sync", lambda e: e.dma_start(out=qT_d, in_=S2[:, :]).then_inc(SD, 16))
    bump("SD", 16)
    p2_sd = cnt["SD"]

    ph = 3

    dec_used, dec_tp, flush_sd = {}, {}, {}
    for mi, (wp_d, ws_d) in enumerate([(gwp, gws), (uwp, uws), (dwp, dws)]):
        for g in range(16):
            b = g % 2
            prev_used = dec_used.get((mi, g - 2)) or (dec_used.get((mi - 1, 14 + g)) if mi > 0 else None)
            if prev_used:
                wait(ph, "sync", VS, prev_used)
            add(ph, "sync", lambda e, b=b, g=g, wp=wp_d: e.dma_start(
                out=pk_t[b], in_=wp[g * 128:(g + 1) * 128, :]).then_inc(SD, 16))
            add(ph, "sync", lambda e, b=b, g=g, ws=ws_d: e.dma_start(
                out=dsc[b], in_=ws[g * 128:(g + 1) * 128, :]).then_inc(SD, 16))
            bump("SD", 32)
            wait(ph, "dve", SD, cnt["SD"])
            prev_tp = dec_tp.get((mi, g - 2)) or (dec_tp.get((mi - 1, 14 + g)) if mi > 0 else None)
            if prev_tp:
                wait(ph, "dve", TS, prev_tp)

            v0 = cnt["VS"]
            def dec(e, b=b, v0=v0):
                t0, t1, t2, t3, t4, t5 = dec_t
                w = pk_t[b]
                v = v0
                def st(inst):
                    nonlocal v
                    v += 1
                    inst.then_inc(VS, 1)
                    e.wait_ge(VS, v)
                st(e.tensor_scalar(t0, w, 7, None, ALU.bitwise_and))
                st(e.tensor_scalar(t1, w, 4, 7, ALU.logical_shift_right, ALU.bitwise_and))
                st(e.tensor_scalar(t2, t0, 64, 16128, ALU.mult, ALU.add))
                st(e.scalar_tensor_tensor(t3, t0, 16128.0, t2, ALU.mult, ALU.min))
                st(e.tensor_scalar(t4, t1, 64, 16128, ALU.mult, ALU.add))
                st(e.scalar_tensor_tensor(t5, t1, 16128.0, t4, ALU.mult, ALU.min))
                st(e.tensor_scalar(t0, w, 8, 12, ALU.bitwise_and, ALU.logical_shift_left))
                st(e.tensor_scalar(t1, w, 128, 8, ALU.bitwise_and, ALU.logical_shift_left))
                st(e.tensor_tensor(t2, t3, t0, ALU.add))
                st(e.tensor_tensor(t4, t5, t1, ALU.add))
                dvv = decb[b].rearrange("p (a c) -> p a c", c=32)
                t2b = t2.bitcast(BF16).rearrange("p (a c) -> p a c", c=32)
                t4b = t4.bitcast(BF16).rearrange("p (a c) -> p a c", c=32)
                scb = dsc[b].unsqueeze(2).broadcast_to([128, 64, 16])
                st(e.scalar_tensor_tensor(dvv[:, :, 0:32:2], t2b[:, :, 0:32:2], 1.0,
                                          scb, ALU.mult, ALU.mult))
                st(e.scalar_tensor_tensor(dvv[:, :, 1:32:2], t4b[:, :, 0:32:2], 1.0,
                                          scb, ALU.mult, ALU.mult))
            add(ph, "dve", dec)
            dec_used[(mi, g)] = bump("VS", 12)
            vs_dec = cnt["VS"]
            for half in range(2):
                bi = half
                wait(ph, "pe", VS, vs_dec)
                wait(ph, "pe", SS, pb_free[bi])
                if mi > 0 and g == 0 and half == 0:
                    wait(ph, "pe", SD, flush_sd[mi - 1])

                def tp(e, b=b, bank=pb[bi], half=half):
                    for i in range(8):
                        dc = half * 8 + i
                        inst = e.transpose(bank[:, i * 128:(i + 1) * 128],
                                           decb[b][:, dc * 128:(dc + 1) * 128], ident)
                    inst.then_inc(TS, 1)
                add(ph, "pe", tp)
                bump("TS")
                wait(ph, "act", TS, cnt["TS"])

                def ev(e, bank=pb[bi], half=half, g=g):
                    for i in range(8):
                        dc = half * 8 + i
                        inst = e.copy(S3[:, dc * 2048 + g * 128: dc * 2048 + g * 128 + 128],
                                      bank[:, i * 128:(i + 1) * 128])
                    inst.then_inc(SS, 1)
                add(ph, "act", ev)
                pb_free[bi] = bump("SS")
            dec_tp[(mi, g)] = cnt["TS"]
        wait(ph, "sync", SS, cnt["SS"])
        add(ph, "sync", lambda e, mi=mi: e.dma_start(
            out=cc1_in[mi].ap(), in_=S3[:, :]).then_inc(SD, 16))
        flush_sd[mi] = bump("SD", 16)
    wait(ph, "gpsimd", SD, flush_sd[2])
    add(ph, "gpsimd", lambda e: e.dma_start(
        out=S3[:, :].rearrange("p (c h) -> p c h", c=16), in_=wo_d.rearrange("(c p) h -> p c h", p=128)).then_inc(GD, 16))
    gd_wo = bump("GD", 16)
    p3_sd = cnt["SD"]

    ph = 4
    for eng in ("pe", "act", "dve", "sync"):
        wait(ph, eng, CS, cs_kv)
    wait(ph, "sync", SD, p3_sd)
    wait(ph, "gpsimd", CS, cs_kv)
    wait(ph, "gpsimd", SD, flush_sd[2])
    for mi in range(3):
        add(ph, "gpsimd", lambda e, mi=mi: e.collective_compute(
            "AllGather", ALU.bypass, replica_groups=[list(range(R))],
            ins=[cc1_in[mi].ap().opt()], outs=[cc1_out[mi].ap().opt()]).then_inc(CS, 1))
        bump("CS")
    cs_w = cnt["CS"]

    att_done, att_evac, att_binv, att_edma, att_recip = {}, {}, {}, {}, {}

    def prev_key(s, h, lag):
        if h - lag >= 0:
            return (s, h - lag)
        if s > 0:
            return (s - 1, 16 + h - lag)
        return None

    for s in range(4):
        lt = SLOT_LT[s]
        kmax = SLOT_KMAX[s]
        mstart = SLOT_MSTART[s]
        runs = kv_runs(kmax)

        def gen_masks(e, s=s, kmax=kmax, mstart=mstart):
            for jm in range(mstart, kmax):
                inst = e.tensor_scalar(
                    masks[:, (jm - mstart) * 512:(jm - mstart + 1) * 512],
                    colq, mrow[:, s * 32 + jm: s * 32 + jm + 1], None, ALU.is_ge)
            inst.then_inc(VS, 1)
        add(ph, "dve", gen_masks)
        mask_vs = bump("VS")

        for h in range(NH):
            bi = h % 2
            kc, vc, qc = att_bufs[bi]
            pk2 = prev_key(s, h, 2)
            pk1 = prev_key(s, h, 1)
            if pk2 and pk2 in att_done:
                wait(ph, "sync", TS, att_done[pk2])
            for (j0, r, tc0, ln) in runs:
                add(ph, "sync", lambda e, j0=j0, r=r, tc0=tc0, ln=ln, kc=kc, h=h: e.dma_start(
                    out=kc[:, j0 * 128:(j0 + ln) * 128],
                    in_=cc2_out.ap()[r * 128:(r + 1) * 128,
                                     h * 2048 + tc0 * 128: h * 2048 + (tc0 + ln) * 128]
                ).then_inc(SD, 16))

                def vload(e, j0=j0, r=r, tc0=tc0, ln=ln, vc=vc, h=h):
                    e.dma_start(
                        out=vc[:, j0 * 128:(j0 + ln) * 128],
                        in_=cc2_out.ap()[r * 128:(r + 1) * 128,
                                         32768 + h * 2048 + tc0 * 128:
                                         32768 + h * 2048 + (tc0 + ln) * 128]
                    ).then_inc(SD, 16)
                add(ph, "sync", vload)
                bump("SD", 32)
            add(ph, "sync", lambda e, qc=qc, h=h, lt=lt: e.dma_start(
                out=qc, in_=qT_d[:, h * 2048 + lt * 512: h * 2048 + lt * 512 + 512]
            ).then_inc(SD, 16))
            dma_tgt = bump("SD", 16)

            av_bank = pf[3 + bi]
            sm_bank = pf[5]
            exp_ss = {}
            av_ts = {}
            pt_of = {}

            def emit_st(j):
                sbank = pf[j % 3]
                if j == 0:
                    wait(ph, "pe", SD, dma_tgt)
                    wait(ph, "pe", VS, mask_vs)
                    pe_evac = att_evac.get(pk2) if pk2 else None
                    if pe_evac:
                        wait(ph, "pe", VE, pe_evac)
                    if pk1 and pk1 in att_recip:
                        wait(ph, "pe", SS, att_recip[pk1])
                if j == 2 and pk1 and pk1 in att_binv:
                    wait(ph, "pe", VE, att_binv[pk1])
                if j >= 3:
                    wait(ph, "pe", SS, exp_ss[j - 3])
                add(ph, "pe", lambda e, sbank=sbank, kc=kc, qc=qc, j=j: e.matmul(
                    sbank[:, :], kc[:, j * 128:(j + 1) * 128], qc,
                    start=True, stop=True).then_inc(TS, 1))
                st_ts = bump("TS")
                wait(ph, "act", TS, st_ts)
                if j >= 3:
                    wait(ph, "act", TS, av_ts[j - 3])
                elif pk1 and pk1 in att_done:
                    wait(ph, "act", TS, att_done[pk1])
                masked = j >= mstart
                add(ph, "act", lambda e, j=j, sbank=sbank: e.activation(
                    ptile[j % 3], sbank[:, :], AF.Exp).then_inc(SS, 1))
                exp_ss[j] = bump("SS")
                if masked:
                    wait(ph, "dve", SS, exp_ss[j])
                    wait(ph, "dve", VS, mask_vs)
                    add(ph, "dve", lambda e, j=j, mstart=mstart: e.tensor_tensor(
                        ptile[j % 3], ptile[j % 3],
                        masks[:, (j - mstart) * 512:(j - mstart + 1) * 512],
                        ALU.mult).then_inc(PT, 1))
                    pt_of[j] = ("PT", bump("PT"))
                else:
                    pt_of[j] = ("SS", exp_ss[j])

            def emit_av(j):
                kind, val = pt_of[j]
                wait(ph, "pe", PT if kind == "PT" else SS, val)

                def av(e, j=j, vc=vc, av_bank=av_bank, sm_bank=sm_bank, kmax=kmax):
                    e.matmul(av_bank[:, :], vc[:, j * 128:(j + 1) * 128], ptile[j % 3],
                             start=(j == 0), stop=(j == kmax - 1))
                    e.matmul(sm_bank[0:1, :], ones_col, ptile[j % 3],
                             start=(j == 0), stop=(j == kmax - 1)).then_inc(TS, 1)
                add(ph, "pe", av)
                av_ts[j] = bump("TS")

            for j in range(kmax):
                emit_st(j)
                if j >= 2:
                    emit_av(j - 2)
            emit_av(kmax - 2)
            emit_av(kmax - 1)
            att_done[(s, h)] = cnt["TS"]
            wait(ph, "act", TS, cnt["TS"])
            s0 = cnt["SS"]
            def lninv(e, sm_bank=sm_bank, s0=s0):
                e.activation(lntmp, sm_bank[0:1, :], AF.Ln).then_inc(SS, 1)
                e.wait_ge(SS, s0 + 1)
                e.activation(inv_sb, lntmp, AF.Exp, scale=-1.0).then_inc(SS, 1)
            add(ph, "act", lninv)
            bump("SS")
            att_recip[(s, h)] = bump("SS")
            wait(ph, "pe", SS, cnt["SS"])
            add(ph, "pe", lambda e: e.matmul(pf[2][:, :], ones_row, inv_sb,
                                             start=True, stop=True).then_inc(TS, 1))
            bump("TS")
            wait(ph, "dve", TS, cnt["TS"])
            add(ph, "dve", lambda e: e.tensor_copy(invb, pf[2][:, :]).then_inc(VE, 1))
            att_binv[(s, h)] = bump("VE")
            wait(ph, "dve", VE, att_binv[(s, h)])
            if pk2 and pk2 in att_edma:
                wait(ph, "dve", SD, att_edma[pk2])
            add(ph, "dve", lambda e, bi=bi, av_bank=av_bank: e.tensor_tensor(
                evT[bi], av_bank[:, :], invb, ALU.mult).then_inc(VE, 1))
            att_evac[(s, h)] = bump("VE")
            wait(ph, "sync", VE, cnt["VE"])
            add(ph, "sync", lambda e, bi=bi, h=h, lt=lt: e.dma_start(
                out=attnT_d[:, h * 2048 + lt * 512: h * 2048 + lt * 512 + 512],
                in_=evT[bi]).then_inc(SD, 16))
            att_edma[(s, h)] = bump("SD", 16)
    p4_sd = cnt["SD"]

    ph = 5
    wait(ph, "sync", SD, p4_sd)
    add(ph, "gpsimd", lambda e: e.dma_start(out=browb, in_=bo_d).then_inc(GD, 16))
    gd_bo = bump("GD", 16)
    p5_cp = {}
    for i in range(4):
        wait(ph, "pe", GD, gd_bo)
        wait(ph, "pe", GD, gd_wo)
        if i >= 2:
            wait(ph, "pe", VS, p5_cp[i - 2])
        else:
            wait(ph, "pe", VS, pf_hist[i])
        add(ph, "pe", lambda e, i=i: e.matmul(
            pf[i % 2][:, :], ones_rowb, browb[0:1, i * 512:(i + 1) * 512],
            start=True, stop=True).then_inc(TS, 1))
        bump("TS")
        wait(ph, "dve", TS, cnt["TS"])
        add(ph, "dve", lambda e, i=i: e.tensor_copy(
            bcast[:, i * 512:(i + 1) * 512], pf[i % 2][:, :]).then_inc(VS, 1))
        p5_cp[i] = pf_hist[i] = bump("VS")
    bo_vs = cnt["VS"]

    p5_mm, p5_ev, p5_ln3, p5_x1sd, p5_tp = {}, {}, {}, {}, {}
    for tt in range(16):
        b = tt % 2
        at = p5_at[b]
        if tt >= 2:
            wait(ph, "sync", TS, p5_mm[tt - 2])
        if tt >= 1:
            wait(ph, "sync", VS, p5_ln3[tt - 1])
            wait(ph, "sync", SD, p5_x1sd[tt - 1])

        def at_load(e, at=at, tt=tt):
            src = attnT_d.rearrange("p (h t) -> p h t", h=NH)[:, :, tt * 128:(tt + 1) * 128]
            e.dma_start(out=at.rearrange("p (h n) -> p h n", h=NH), in_=src).then_inc(SD, 16)
        add(ph, "sync", at_load)
        add(ph, "sync", lambda e, tt=tt: e.dma_start(
            out=p5_x, in_=xo[tt * 128:(tt + 1) * 128, :]).then_inc(SD, 16))
        x_sd = bump("SD", 32)
        for dt in range(4):
            idx = tt * 4 + dt
            bankk = 2 + idx % 2
            bank = pf[bankk]
            if idx == 0:
                wait(ph, "pe", VS, bo_vs)
            wait(ph, "pe", SD, x_sd)
            if idx >= 2:
                wait(ph, "pe", VS, p5_ev[idx - 2])
            else:
                wait(ph, "pe", VS, pf_hist[bankk])

            def mm(e, at=at, bank=bank, dt=dt):
                for hc in range(NH):
                    inst = e.matmul(bank[:, :], at[:, hc * 128:(hc + 1) * 128],
                                    S3[:, hc * 2048 + dt * 512: hc * 2048 + (dt + 1) * 512],
                                    start=(hc == 0), stop=(hc == NH - 1))
                inst.then_inc(TS, 1)
            add(ph, "pe", mm)
            bump("TS")
            wait(ph, "dve", TS, cnt["TS"])
            if dt == 0 and tt >= 1:
                wait(ph, "dve", VS, p5_ln3[tt - 1])
            add(ph, "dve", lambda e, bank=bank, dt=dt: e.tensor_tensor(
                p5_y[:, dt * 512:(dt + 1) * 512], bank[:, :],
                bcast[:, dt * 512:(dt + 1) * 512], ALU.add).then_inc(VS, 1))
            p5_ev[idx] = pf_hist[bankk] = bump("VS")
        p5_mm[tt] = cnt["TS"]
        wait(ph, "dve", VS, p5_ev[tt * 4 + 3])
        ln_stats(ph, p5_y)

        v0 = cnt["VS"]
        def ln2_apply(e, v0=v0):
            e.tensor_scalar(p5_y, p5_y, mv[:, 0:1], None, ALU.subtract).then_inc(VS, 1)
            e.wait_ge(VS, v0 + 1)
            e.tensor_scalar(p5_y, p5_y, rstd, None, ALU.mult).then_inc(VS, 1)
            e.wait_ge(VS, v0 + 2)
            e.tensor_tensor(p5_x, p5_x, p5_y, ALU.add).then_inc(VS, 1)
        add(ph, "dve", ln2_apply)
        bump("VS", 3)
        wait(ph, "sync", VS, cnt["VS"])
        add(ph, "sync", lambda e, tt=tt: e.dma_start(
            out=x1_d[tt * 128:(tt + 1) * 128, :], in_=p5_x).then_inc(SD, 16))
        p5_x1sd[tt] = bump("SD", 16)
        ln_stats(ph, p5_x)
        if tt >= 1:
            wait(ph, "dve", TS, p5_tp[tt - 1])

        v0 = cnt["VS"]
        def ln3_apply(e, v0=v0):
            e.tensor_scalar(p5_y, p5_x, mv[:, 0:1], None, ALU.subtract).then_inc(VS, 1)
            e.wait_ge(VS, v0 + 1)
            e.tensor_scalar(p5_xn, p5_y, rstd, None, ALU.mult).then_inc(VS, 1)
        add(ph, "dve", ln3_apply)
        bump("VS")
        p5_ln3[tt] = bump("VS")
        wait(ph, "pe", VS, cnt["VS"])
        p5_tp[tt] = transposes(ph, p5_xn, S1, tt)
    p5_sd = cnt["SD"]

    ph = 6
    wait(ph, "sync", CS, cs_w)
    wait(ph, "sync", SD, p5_sd)

    gu_ts = {}
    inter_vs = {}
    t3_vs = {}
    wdma = {}
    down_ts = {}
    down_hist = {4: pf_hist[4], 5: pf_hist[5]}
    flush_slice = {}
    out_add_vs = {}
    out_so = {}
    n_out = 0

    def emit_down_group(to, g8):
        dg = to * 16 + g8
        last_gidx = to * 128 + g8 * 8 + 7
        for ts4 in range(4):
            for ht in range(4):
                q = ts4 * 4 + ht
                bankk = 4 + q % 2
                bank = pf[bankk]
                if q == 0:
                    wait(ph, "pe", VS, inter_vs[last_gidx])
                    wait(ph, "pe", SD, wd_sd[dg])
                wait(ph, "pe", VS, down_hist[bankk])

                def dmm(e, ts4=ts4, ht=ht, bank=bank, g8=g8):
                    for i8 in range(8):
                        sl = (g8 * 8 + i8) % 16
                        inst = e.matmul(
                            bank[:, :],
                            it_ring[sl][:, ts4 * 128:(ts4 + 1) * 128],
                            wd_ring[i8][:, ht * 512:(ht + 1) * 512],
                            start=(i8 == 0), stop=(i8 == 7))
                    inst.then_inc(TS, 1)
                add(ph, "pe", dmm)
                bump("TS")
                wait(ph, "dve", TS, cnt["TS"])
                pfl = flush_slice.get((ts4, ht))
                if pfl:
                    wait(ph, "dve", VS, pfl)
                if g8 == 0:
                    add(ph, "dve", lambda e, bank=bank, ts4=ts4, ht=ht: e.tensor_copy(
                        acc[:, ts4 * 2048 + ht * 512: ts4 * 2048 + (ht + 1) * 512],
                        bank[:, :]).then_inc(VS, 1))
                else:
                    add(ph, "dve", lambda e, bank=bank, ts4=ts4, ht=ht: e.tensor_tensor(
                        acc[:, ts4 * 2048 + ht * 512: ts4 * 2048 + (ht + 1) * 512],
                        acc[:, ts4 * 2048 + ht * 512: ts4 * 2048 + (ht + 1) * 512],
                        bank[:, :], ALU.add).then_inc(VS, 1))
                down_hist[bankk] = flush_slice[(ts4, ht)] = bump("VS")
        down_ts[dg] = cnt["TS"]

    def emit_wd_load(to, c):
        g_of_c = c // 8
        dgp = to * 16 + g_of_c - 1
        if dgp >= 0 and dgp in down_ts:
            wait(ph, "sync", TS, down_ts[dgp])
        r, o = c // 16, c % 16
        add(ph, "sync", lambda e, r=r, o=o, ring_i=c % 8: e.dma_start(
            out=wd_ring[ring_i],
            in_=cc1_out[2].ap()[r * 128:(r + 1) * 128,
                                o * 2048:(o + 1) * 2048]).then_inc(SD, 16))
        wd_sd[to * 16 + g_of_c] = bump("SD", 16)

    wd_sd = {}
    for to in range(4):
        for nci in range(128):
            r, o = nci // 16, nci % 16
            buf = nci % 2
            ring_i = nci % 16
            g8 = nci // 8
            gidx = to * 128 + nci
            if gidx >= 2:
                wait(ph, "sync", TS, gu_ts[gidx - 2])

            def wload(e, r=r, o=o, buf=buf):
                src_g = cc1_out[0].ap()[r * 128:(r + 1) * 128, :]
                src_g = src_g.rearrange("p (c n) -> p c n", c=16)[:, :, o * 128:(o + 1) * 128]
                e.dma_start(out=wg_t[buf].rearrange("p (c n) -> p c n", c=16),
                            in_=src_g).then_inc(SD, 16)
                src_u = cc1_out[1].ap()[r * 128:(r + 1) * 128, :]
                src_u = src_u.rearrange("p (c n) -> p c n", c=16)[:, :, o * 128:(o + 1) * 128]
                e.dma_start(out=wu_t[buf].rearrange("p (c n) -> p c n", c=16),
                            in_=src_u).then_inc(SD, 16)
            add(ph, "sync", wload)
            wdma[gidx] = bump("SD", 32)
            if nci >= 8:
                emit_wd_load(to, nci - 8)

            wait(ph, "pe", SD, wdma[gidx])
            if gidx >= 2:
                wait(ph, "pe", VS, inter_vs[gidx - 2])
            else:
                wait(ph, "pe", VS, max(pf_hist[buf], pf_hist[2 + buf]))

            def gmm(e, buf=buf, to=to):
                for c in range(NDC):
                    inst = e.matmul(pf[buf][:, :], wg_t[buf][:, c * 128:(c + 1) * 128],
                                    S1[:, c * 2048 + to * 512: c * 2048 + (to + 1) * 512],
                                    start=(c == 0), stop=(c == NDC - 1))
                inst.then_inc(TS, 1)
            add(ph, "pe", gmm)
            g_ts = bump("TS")

            def umm(e, buf=buf, to=to):
                for c in range(NDC):
                    inst = e.matmul(pf[2 + buf][:, :], wu_t[buf][:, c * 128:(c + 1) * 128],
                                    S1[:, c * 2048 + to * 512: c * 2048 + (to + 1) * 512],
                                    start=(c == 0), stop=(c == NDC - 1))
                inst.then_inc(TS, 1)
            add(ph, "pe", umm)
            gu_ts[gidx] = bump("TS")

            if nci % 8 == 7 and g8 >= 1:
                emit_down_group(to, g8 - 1)

            wait(ph, "act", TS, g_ts)
            if gidx >= 1:
                wait(ph, "act", VS, inter_vs[gidx - 1])
            add(ph, "act", lambda e, buf=buf: e.activation(
                g_sq, pf[buf][:, :], AF.Square).then_inc(SS, 1))
            sq_ss = bump("SS")
            wait(ph, "dve", SS, sq_ss)
            v0 = cnt["VS"]
            add(ph, "dve", lambda e, v0=v0: (
                e.tensor_scalar(g_t2, g_sq, 0.044715 * 0.7978845608028654,
                                0.7978845608028654, ALU.mult, ALU.add).then_inc(VS, 1),
                e.wait_ge(VS, v0 + 1)))
            bump("VS")
            add(ph, "dve", lambda e, buf=buf: e.tensor_tensor(
                g_t2, g_t2, pf[buf][:, :], ALU.mult).then_inc(VS, 1))
            t3_vs[gidx] = bump("VS")
            wait(ph, "act", VS, t3_vs[gidx])
            add(ph, "act", lambda e: e.activation(
                g_th, g_t2, AF.Tanh).then_inc(SS, 1))
            th_ss = bump("SS")
            wait(ph, "dve", SS, th_ss)
            v0 = cnt["VS"]
            add(ph, "dve", lambda e, buf=buf, v0=v0: (
                e.scalar_tensor_tensor(g_b, g_th, 1.0, pf[buf][:, :],
                                       ALU.add, ALU.mult).then_inc(VS, 1),
                e.wait_ge(VS, v0 + 1)))
            bump("VS")
            wait(ph, "dve", TS, gu_ts[gidx])
            add(ph, "dve", lambda e, buf=buf, ring_i=ring_i: e.scalar_tensor_tensor(
                it_ring[ring_i], g_b, 0.5, pf[2 + buf][:, :],
                ALU.mult, ALU.mult).then_inc(VS, 1))
            inter_vs[gidx] = pf_hist[buf] = pf_hist[2 + buf] = bump("VS")

        for c in range(120, 128):
            emit_wd_load(to, c)
        emit_down_group(to, 15)

        for ts4 in range(4):
            acc_sl = acc[:, ts4 * 2048:(ts4 + 1) * 2048]
            pk = out_add_vs.get((to - 1, ts4))
            if pk:
                wait(ph, "gpsimd", VS, pk)
            add(ph, "gpsimd", lambda e, to=to, ts4=ts4: e.dma_start(
                out=p6_x1, in_=x1_d[to * 512 + ts4 * 128: to * 512 + (ts4 + 1) * 128, :]
            ).then_inc(GD, 16))
            x1_sd = bump("GD", 16)
            wait(ph, "dve", VS, flush_slice[(ts4, 3)])
            ln_stats(ph, acc_sl)

            v0 = cnt["VS"]
            def ln4_apply(e, acc_sl=acc_sl, v0=v0):
                e.tensor_scalar(acc_sl, acc_sl, mv[:, 0:1], None,
                                ALU.subtract).then_inc(VS, 1)
                e.wait_ge(VS, v0 + 1)
                e.tensor_scalar(acc_sl, acc_sl, rstd, None, ALU.mult).then_inc(VS, 1)
            add(ph, "dve", ln4_apply)
            bump("VS", 2)
            wait(ph, "dve", GD, x1_sd)
            if n_out >= 1:
                wait(ph, "dve", SO, out_so[n_out - 1])
            add(ph, "dve", lambda e, acc_sl=acc_sl: e.tensor_tensor(
                p6_o, acc_sl, p6_x1, ALU.add).then_inc(VS, 1))
            out_add_vs[(to, ts4)] = bump("VS")
            wait(ph, "gpsimd", VS, cnt["VS"])
            add(ph, "gpsimd", lambda e, to=to, ts4=ts4: e.dma_start(
                out=out_d[to * 512 + ts4 * 128: to * 512 + (ts4 + 1) * 128, :],
                in_=p6_o).then_inc(SO, 16))
            out_so[n_out] = bump("SO", 16)
            n_out += 1

    wait(ph, "sync", SO, cnt["SO"])
    if dbg:
        add(ph, "sync", lambda e: e.dma_start(out=d_qT, in_=qT_d).then_inc(SO, 16))
        add(ph, "sync", lambda e: e.dma_start(out=d_kv, in_=cc2_out.ap()).then_inc(SO, 16))
        add(ph, "sync", lambda e: e.dma_start(out=d_att, in_=attnT_d).then_inc(SO, 16))
        add(ph, "sync", lambda e: e.dma_start(out=d_x1, in_=x1_d).then_inc(SO, 16))
        add(ph, "sync", lambda e: e.dma_start(
            out=d_w, in_=cc1_out.ap()[:, 0:2048]).then_inc(SO, 16))
        add(ph, "sync", lambda e: e.dma_start(
            out=d_wd, in_=cc1_out.ap()[:, 65536:67584]).then_inc(SO, 16))
        bump("SO", 96)
        wait(ph, "sync", SO, cnt["SO"])

    def runner(lst):
        return lambda e: [f(e) for f in lst]

    with es:
        for p in range(NPH):
            with nc.Block() as blk:
                blk.tensor(runner(streams[p]["pe"]))
                blk.vector(runner(streams[p]["dve"]))
                blk.scalar(runner(streams[p]["act"]))
                blk.gpsimd(runner(streams[p]["gpsimd"]))
                blk.sync(runner(streams[p]["sync"]))
    return nc


_NC_CACHE = [None]
DBG = [False]
_LAST_RES = [None]


def kernel(L0_attn_x, L0_attn_Wq, L0_attn_bq, L0_attn_Wk, L0_attn_bk,
           L0_attn_Wv, L0_attn_bv, L0_attn_Wo, L0_attn_bo,
           L0_ffn_gate_W_packed, L0_ffn_gate_scales,
           L0_ffn_up_W_packed, L0_ffn_up_scales,
           L0_ffn_down_W_packed, L0_ffn_down_scales):
    x = np.ascontiguousarray(np.asarray(L0_attn_x, dtype=np.float32))
    wq = np.ascontiguousarray(np.asarray(L0_attn_Wq, dtype=np.float32))
    wk = np.ascontiguousarray(np.asarray(L0_attn_Wk, dtype=np.float32))
    wv = np.ascontiguousarray(np.asarray(L0_attn_Wv, dtype=np.float32))
    wo = np.ascontiguousarray(np.asarray(L0_attn_Wo, dtype=np.float32))
    bq = np.asarray(L0_attn_bq, dtype=np.float32)
    bk = np.asarray(L0_attn_bk, dtype=np.float32)
    bv = np.asarray(L0_attn_bv, dtype=np.float32)
    bo = np.asarray(L0_attn_bo, dtype=np.float32)
    gp = np.asarray(L0_ffn_gate_W_packed, dtype=np.int32)
    gs = np.asarray(L0_ffn_gate_scales, dtype=np.float32)
    up = np.asarray(L0_ffn_up_W_packed, dtype=np.int32)
    us = np.asarray(L0_ffn_up_scales, dtype=np.float32)
    dp = np.asarray(L0_ffn_down_W_packed, dtype=np.int32)
    ds = np.asarray(L0_ffn_down_scales, dtype=np.float32)

    bq_r = np.ascontiguousarray(bq.reshape(NDC, 128).T)
    bk_r = np.ascontiguousarray(bk.reshape(NDC, 128).T)
    bv_row = np.ascontiguousarray(bv.reshape(1, D))
    bo_row = np.ascontiguousarray(bo.reshape(1, D))

    in_maps = []
    for c in range(R):
        b = c // 2
        half = c % 2
        blocks = [0, 3] if half == 0 else [1, 2]
        x_own = np.concatenate([x[b, blk * 1024:(blk + 1) * 1024, :] for blk in blocks],
                               axis=0)
        qrows = []
        for h in range(NH):
            for blk in blocks:
                qrows.extend(range(h * 256 + blk * 64, h * 256 + blk * 64 + 64))
        xq_own = x[b][np.asarray(qrows)]
        maskrow = np.zeros((128, 128), dtype=np.float32)
        si = np.arange(128, dtype=np.float32)[:, None]
        for s in range(4):
            lt = SLOT_LT[s]
            qg0 = QG0[half][lt]
            for j in range(32):
                maskrow[:, s * 32 + j: s * 32 + j + 1] = si + (j * 128 - qg0)
        in_maps.append({
            "xo": np.ascontiguousarray(x_own),
            "xq": np.ascontiguousarray(xq_own),
            "wq": wq, "wk": wk, "wv": wv, "wo": wo,
            "bq_r": bq_r, "bk_r": bk_r, "bv_row": bv_row, "bo_row": bo_row,
            "gwp": np.ascontiguousarray(gp[c * 2048:(c + 1) * 2048, :]),
            "gws": np.ascontiguousarray(gs[c * 2048:(c + 1) * 2048, :]),
            "uwp": np.ascontiguousarray(up[c * 2048:(c + 1) * 2048, :]),
            "uws": np.ascontiguousarray(us[c * 2048:(c + 1) * 2048, :]),
            "dwp": np.ascontiguousarray(dp[:, c * 1024:(c + 1) * 1024]),
            "dws": np.ascontiguousarray(ds[:, c * 64:(c + 1) * 64]),
            "maskrow": maskrow,
        })

    if _NC_CACHE[0] is None:
        _NC_CACHE[0] = build_program(dbg=DBG[0])
    nc = _NC_CACHE[0]
    res = run_bass_kernel_spmd(nc, in_maps, core_ids=list(range(R)))
    if DBG[0]:
        _LAST_RES[0] = res

    out = np.empty((B_, T_, D), dtype=np.float32)
    for c in range(R):
        b = c // 2
        half = c % 2
        blocks = [0, 3] if half == 0 else [1, 2]
        o = np.asarray(res.results[c]["out"], dtype=np.float32)
        for i, blk in enumerate(blocks):
            out[b, blk * 1024:(blk + 1) * 1024, :] = o[i * 1024:(i + 1) * 1024, :]
    return out


# version 18
# speedup vs baseline: 1.0675x; 1.0675x over previous
import contextlib
from collections import defaultdict

import numpy as np

import concourse.bass as bass
import concourse.mybir as mybir
from concourse.bass_utils import run_bass_kernel_spmd

F32 = mybir.dt.float32
BF16 = mybir.dt.bfloat16
I32 = mybir.dt.int32
U16 = mybir.dt.uint16
AF = mybir.ActivationFunctionType
ALU = mybir.AluOpType

R = 8
B_, T_, D = 4, 4096, 2048
NH = 16
TOK = 2048
NDC = 16
INV_SQRT_DH = float(np.float32(1.0) / np.float32(np.sqrt(np.float32(128.0))))
EPS = 1e-5

SLOT_LT = [3, 2, 1, 0]
SLOT_KMAX = [32, 28, 16, 12]
SLOT_MSTART = [20, 16, 4, 0]
N_MASK = 12
QG0 = {0: [0, 512, 3072, 3584], 1: [1024, 1536, 2048, 2560]}


def chunk_map(j):
    blk = j // 8
    r = [0, 1, 1, 0][blk]
    tc = (j % 8) + (8 if blk >= 2 else 0)
    return r, tc


def kv_runs(kmax):
    runs, j = [], 0
    while j < kmax:
        r, tc = chunk_map(j)
        ln = 1
        while j + ln < kmax:
            r2, tc2 = chunk_map(j + ln)
            if r2 == r and tc2 == tc + ln:
                ln += 1
            else:
                break
        runs.append((j, r, tc, ln))
        j += ln
    return runs


def build_program(dbg=False):
    nc = bass.Bass("TRN2", target_bir_lowering=False)

    xo = nc.dram_tensor("xo", [TOK, D], F32, kind="ExternalInput").ap()
    xq_d = nc.dram_tensor("xq", [TOK, D], F32, kind="ExternalInput").ap()
    wq_d = nc.dram_tensor("wq", [D, D], F32, kind="ExternalInput").ap()
    wk_d = nc.dram_tensor("wk", [D, D], F32, kind="ExternalInput").ap()
    wv_d = nc.dram_tensor("wv", [D, D], F32, kind="ExternalInput").ap()
    wo_d = nc.dram_tensor("wo", [D, D], F32, kind="ExternalInput").ap()
    bq_d = nc.dram_tensor("bq_r", [128, NDC], F32, kind="ExternalInput").ap()
    bk_d = nc.dram_tensor("bk_r", [128, NDC], F32, kind="ExternalInput").ap()
    bv_d = nc.dram_tensor("bv_row", [1, D], F32, kind="ExternalInput").ap()
    bo_d = nc.dram_tensor("bo_row", [1, D], F32, kind="ExternalInput").ap()
    gwp = nc.dram_tensor("gwp", [2048, 1024], I32, kind="ExternalInput").ap()
    gws = nc.dram_tensor("gws", [2048, 64], F32, kind="ExternalInput").ap()
    uwp = nc.dram_tensor("uwp", [2048, 1024], I32, kind="ExternalInput").ap()
    uws = nc.dram_tensor("uws", [2048, 64], F32, kind="ExternalInput").ap()
    dwp = nc.dram_tensor("dwp", [2048, 1024], I32, kind="ExternalInput").ap()
    dws = nc.dram_tensor("dws", [2048, 64], F32, kind="ExternalInput").ap()
    mrow_d = nc.dram_tensor("maskrow", [128, 128], F32, kind="ExternalInput").ap()
    out_d = nc.dram_tensor("out", [TOK, D], F32, kind="ExternalOutput").ap()

    qT_d = nc.dram_tensor("qT_d", [128, NH * TOK], BF16).ap()
    attnT_d = nc.dram_tensor("attnT_d", [128, NH * TOK], BF16).ap()
    x1_d = nc.dram_tensor("x1_d", [TOK, D], F32).ap()
    cc2_in = nc.dram_tensor("cc2_in", [128, 2 * 32768], BF16)
    cc2_out = nc.dram_tensor("cc2_out", [256, 2 * 32768], BF16)
    cc1_in = [nc.dram_tensor(f"cc1_in{m}", [128, 32768], BF16) for m in range(3)]
    cc1_out = [nc.dram_tensor(f"cc1_out{m}", [128 * R, 32768], BF16, addr_space="Shared")
               for m in range(3)]
    if dbg:
        d_qT = nc.dram_tensor("d_qT", [128, NH * TOK], BF16, kind="ExternalOutput").ap()
        d_kv = nc.dram_tensor("d_kv", [256, 2 * 32768], BF16, kind="ExternalOutput").ap()
        d_att = nc.dram_tensor("d_att", [128, NH * TOK], BF16, kind="ExternalOutput").ap()
        d_x1 = nc.dram_tensor("d_x1", [TOK, D], F32, kind="ExternalOutput").ap()
        d_w = nc.dram_tensor("d_w", [1024, 2048], BF16, kind="ExternalOutput").ap()
        d_wd = nc.dram_tensor("d_wd", [1024, 2048], BF16, kind="ExternalOutput").ap()

    es = contextlib.ExitStack()
    S1 = es.enter_context(nc.sbuf_tensor("S1", [128, 32768], BF16))
    S2 = es.enter_context(nc.sbuf_tensor("S2", [128, 32768], BF16))
    S3 = es.enter_context(nc.sbuf_tensor("S3", [128, 32768], BF16))
    S4 = es.enter_context(nc.sbuf_tensor("S4", [128, 8096], BF16))
    pf = [es.enter_context(nc.psum_tensor(f"pf{i}", [128, 512], F32)) for i in range(6)]
    pb = [es.enter_context(nc.psum_tensor(f"pb{i}", [128, 1024], BF16)) for i in range(2)]

    names = ["SD", "SO", "GD", "CS", "TS", "VS", "SS", "PT", "VE", "GI"]
    sems = {n: es.enter_context(nc.semaphore(n)) for n in names}
    SD, SO, GD, CS, TS, VS, SS, PT, VE, GI = (sems[n] for n in names)
    cnt = {n: 0 for n in names}

    def bump(n, k=1):
        cnt[n] += k
        return cnt[n]

    def carve(slab, boff, p, n, dt):
        esz = 2 if dt in (BF16, U16) else 4
        c0 = boff // 2
        return slab[0:p, c0: c0 + (n * esz) // 2].bitcast(dt)

    stage = [carve(S4, 0, 128, 512, BF16), carve(S4, 1024, 128, 512, BF16)]
    bcast = carve(S4, 2048, 128, 2048, BF16)
    colq = carve(S4, 6144, 128, 512, F32)
    rowi = carve(S4, 8192, 128, 1, F32)
    ident = carve(S4, 8196, 128, 128, BF16)
    ones_col = carve(S4, 8452, 128, 1, BF16)
    mrow = carve(S4, 8456, 128, 128, F32)
    bq_sb = carve(S4, 8968, 128, NDC, F32)
    bk_sb = carve(S4, 9032, 128, NDC, F32)
    stats = carve(S4, 9096, 128, 24, F32)
    mv = carve(S4, 9192, 128, 2, F32)
    sstd = carve(S4, 9200, 128, 1, F32)
    rstd = carve(S4, 9204, 128, 1, F32)
    ones_row = carve(S4, 9208, 1, 128, F32)
    ones_rowb = carve(S4, 9720, 1, 128, BF16)
    inv_sb = carve(S4, 9976, 1, 512, F32)
    browb = carve(S4, 12024, 1, 2048, BF16)
    eps_c = carve(S4, 16120, 128, 1, F32)

    xT = [carve(S2, 0, 128, 2048, F32), carve(S2, 8192, 128, 2048, F32)]
    xn_t = [carve(S2, 16384, 128, 2048, BF16), carve(S2, 20480, 128, 2048, BF16)]

    def att_c(b):
        base = b * 17408
        return (carve(S2, base, 128, 4096, BF16),
                carve(S2, base + 8192, 128, 4096, BF16),
                carve(S2, base + 16384, 128, 512, BF16))
    att_bufs = [att_c(0), att_c(1)]
    masks = carve(S2, 34816, 128, N_MASK * 512, BF16)
    evT = [carve(S2, 47104, 128, 512, BF16), carve(S2, 48128, 128, 512, BF16)]
    ptile = [carve(S2, 49152, 128, 512, BF16), carve(S2, 50176, 128, 512, BF16),
             carve(S2, 53248, 128, 512, BF16)]
    invb = carve(S2, 51200, 128, 512, F32)
    lntmp = carve(S2, 55296, 1, 512, F32)

    p5_at = [carve(S2, 0, 128, 2048, BF16), carve(S2, 4096, 128, 2048, BF16)]
    p5_y = carve(S2, 8192, 128, 2048, F32)
    p5_x = carve(S2, 16384, 128, 2048, F32)
    p5_xn = carve(S2, 24576, 128, 2048, BF16)

    acc = carve(S2, 0, 128, 4 * 2048, F32)
    wd_ring = [carve(S2, 32768 + i * 4096, 128, 2048, BF16) for i in range(8)]

    dec_t = [carve(S1, i * 4096, 128, 1024, I32) for i in range(6)]
    pk_t = [carve(S1, 24576, 128, 1024, I32), carve(S1, 28672, 128, 1024, I32)]
    decb = [carve(S1, 32768, 128, 2048, BF16), carve(S1, 36864, 128, 2048, BF16)]
    dsc = [carve(S1, 40960, 128, 64, F32), carve(S1, 41216, 128, 64, F32)]

    wg_t = [carve(S3, 0, 128, 2048, BF16), carve(S3, 4096, 128, 2048, BF16)]
    wu_t = [carve(S3, 8192, 128, 2048, BF16), carve(S3, 12288, 128, 2048, BF16)]
    it_ring = [carve(S3, 16384 + i * 1024, 128, 512, BF16) for i in range(16)]
    g_sq = carve(S3, 32768, 128, 512, F32)
    g_t2 = carve(S3, 34816, 128, 512, F32)
    g_th = carve(S3, 36864, 128, 512, F32)
    g_b = carve(S3, 38912, 128, 512, F32)
    p6_x1 = carve(S3, 40960, 128, 2048, F32)
    p6_o = carve(S3, 49152, 128, 2048, F32)

    NPH = 7
    streams = [defaultdict(list) for _ in range(NPH)]

    def add(ph, eng, fn):
        streams[ph][eng].append(fn)

    def wait(ph, eng, sem, val):
        if val and val > 0:
            add(ph, eng, lambda e, s=sem, v=val: e.wait_ge(s, v))

    ph = 0
    add(ph, "sync", lambda e: e.dma_start(out=bq_sb, in_=bq_d).then_inc(SD, 16))
    add(ph, "sync", lambda e: e.dma_start(out=bk_sb, in_=bk_d).then_inc(SD, 16))
    add(ph, "sync", lambda e: e.dma_start(out=mrow, in_=mrow_d).then_inc(SD, 16))
    sd_p0 = bump("SD", 48)
    add(ph, "gpsimd", lambda e: e.dma_start(out=browb, in_=bv_d).then_inc(GD, 16))
    gd_bv = bump("GD", 16)
    add(ph, "gpsimd", lambda e: e.iota(colq, pattern=[[1, 512]], base=0,
                                       channel_multiplier=0,
                                       allow_small_or_imprecise_dtypes=True))
    add(ph, "gpsimd", lambda e: e.iota(rowi, pattern=[[0, 1]], base=0,
                                       channel_multiplier=1,
                                       allow_small_or_imprecise_dtypes=True).then_inc(GI, 1))
    bump("GI")
    wait(ph, "dve", GI, cnt["GI"])
    add(ph, "dve", lambda e: e.tensor_scalar(ident, colq[:, 0:128], rowi, None, ALU.is_equal))
    add(ph, "dve", lambda e: e.memset(ones_col, 1.0))
    add(ph, "dve", lambda e: e.memset(eps_c, 1e-5))
    add(ph, "dve", lambda e: e.memset(ones_rowb, 1.0))
    add(ph, "dve", lambda e: e.memset(ones_row, 1.0).then_inc(VS, 1))
    bump("VS")
    vs_ones = cnt["VS"]
    p0_cp = {}
    for i in range(4):
        wait(ph, "pe", GD, gd_bv)
        wait(ph, "pe", VS, vs_ones)
        if i >= 2:
            wait(ph, "pe", VS, p0_cp[i - 2])
        add(ph, "pe", lambda e, i=i: e.matmul(
            pf[i % 2][:, :], ones_rowb, browb[0:1, i * 512:(i + 1) * 512],
            start=True, stop=True).then_inc(TS, 1))
        bump("TS")
        wait(ph, "dve", TS, cnt["TS"])
        add(ph, "dve", lambda e, i=i: e.tensor_copy(
            bcast[:, i * 512:(i + 1) * 512], pf[i % 2][:, :]).then_inc(VS, 1))
        p0_cp[i] = bump("VS")
    pf_hist = {0: cnt["VS"], 1: cnt["VS"], 2: 0, 3: 0, 4: 0, 5: 0}

    def ln_stats(ph, src_f32):
        v0 = cnt["VS"]
        def f(e, s=src_f32, v0=v0):
            for i in range(4):
                inst = e.bn_stats(stats[:, i * 6:(i + 1) * 6], s[:, i * 512:(i + 1) * 512])
            inst.then_inc(VS, 1)
            e.wait_ge(VS, v0 + 1)
            e.bn_aggr(mv, stats).then_inc(VS, 1)
        add(ph, "dve", f)
        bump("VS", 2)
        wait(ph, "act", VS, cnt["VS"])
        add(ph, "act", lambda e: e.activation(sstd, mv[:, 1:2], AF.Sqrt,
                                              bias=eps_c).then_inc(SS, 1))
        bump("SS")
        wait(ph, "dve", SS, cnt["SS"])
        v1 = cnt["VS"]
        add(ph, "dve", lambda e, v1=v1: (e.reciprocal(rstd, sstd).then_inc(VS, 1),
                                         e.wait_ge(VS, v1 + 1)))
        bump("VS")

    pb_free = [0, 0]

    def transposes(ph, src_bf16, dst_slab, tt):
        for half in range(2):
            bi = half
            wait(ph, "pe", SS, pb_free[bi])

            def tp(e, src=src_bf16, bank=pb[bi], half=half):
                for i in range(8):
                    dc = half * 8 + i
                    inst = e.transpose(bank[:, i * 128:(i + 1) * 128],
                                       src[:, dc * 128:(dc + 1) * 128], ident)
                inst.then_inc(TS, 1)
            add(ph, "pe", tp)
            bump("TS")
            wait(ph, "act", TS, cnt["TS"])

            def ev(e, bank=pb[bi], half=half, dst=dst_slab, tt=tt):
                for i in range(8):
                    dc = half * 8 + i
                    inst = e.copy(dst[:, dc * 2048 + tt * 128: dc * 2048 + tt * 128 + 128],
                                  bank[:, i * 128:(i + 1) * 128])
                inst.then_inc(SS, 1)
            add(ph, "act", ev)
            pb_free[bi] = bump("SS")
        return cnt["TS"]

    ph = 1
    add(ph, "gpsimd", lambda e: e.dma_start(
        out=S3[:, :].rearrange("p (c h) -> p c h", c=16), in_=wk_d.rearrange("(c p) h -> p c h", p=128)).then_inc(GD, 16))
    gd_wk = bump("GD", 16)

    def ln_pass(ph, src_dram):
        xfree = [0] * 16
        src_done = [0] * 16
        for tt in range(16):
            b = tt % 2
            if tt >= 2:
                wait(ph, "sync", VS, xfree[tt - 2])
            add(ph, "sync", lambda e, b=b, tt=tt, s=src_dram: e.dma_start(
                out=xT[b], in_=s[tt * 128:(tt + 1) * 128, :]).then_inc(SD, 16))
            bump("SD", 16)
            wait(ph, "dve", SD, cnt["SD"])
            ln_stats(ph, xT[b])
            if tt >= 2:
                wait(ph, "dve", TS, src_done[tt - 2])

            v0 = cnt["VS"]
            def app(e, b=b, v0=v0):
                e.tensor_scalar(xT[b], xT[b], mv[:, 0:1], None,
                                ALU.subtract).then_inc(VS, 1)
                e.wait_ge(VS, v0 + 1)
                e.tensor_scalar(xn_t[b], xT[b], rstd, None, ALU.mult).then_inc(VS, 1)
            add(ph, "dve", app)
            bump("VS")
            xfree[tt] = bump("VS")
            wait(ph, "pe", VS, cnt["VS"])
            src_done[tt] = transposes(ph, xn_t[b], S1, tt)

    ln_pass(ph, xo)

    ph = 2
    stage_dma = [0, 0]
    stage_use = [0]

    def proj_pass(ph, wslab, bias_sb, scale, dst_ap, dst_off, tokmajor, pfb, gd_need,
                  vhead=False):
        evac = {}
        wait(ph, "dve", SD, sd_p0)
        wait(ph, "dve", GD, gd_bv)
        for a in range(16):
            for b4 in range(4):
                idx = a * 4 + b4
                bankk = pfb + idx % 2
                bank = pf[bankk]
                if idx < 2:
                    wait(ph, "pe", GD, gd_need)
                    wait(ph, "pe", VS, pf_hist[bankk])
                else:
                    wait(ph, "pe", VS, evac[idx - 2])
                if tokmajor:
                    def mm(e, a=a, b4=b4, bank=bank, ws=wslab):
                        for dc in range(NDC):
                            inst = e.matmul(
                                bank[:, :],
                                S1[:, dc * 2048 + a * 128: dc * 2048 + (a + 1) * 128],
                                ws[:, dc * 2048 + b4 * 512: dc * 2048 + (b4 + 1) * 512],
                                start=(dc == 0), stop=(dc == NDC - 1))
                        inst.then_inc(TS, 1)
                else:
                    def mm(e, a=a, b4=b4, bank=bank, ws=wslab):
                        for dc in range(NDC):
                            inst = e.matmul(
                                bank[:, :],
                                ws[:, dc * 2048 + a * 128: dc * 2048 + (a + 1) * 128],
                                S1[:, dc * 2048 + b4 * 512: dc * 2048 + (b4 + 1) * 512],
                                start=(dc == 0), stop=(dc == NDC - 1))
                        inst.then_inc(TS, 1)
                add(ph, "pe", mm)
                bump("TS")
                su = stage_use[0]
                sb = stage[su % 2]
                wait(ph, "dve", TS, cnt["TS"])
                wait(ph, "dve", SD, stage_dma[su % 2])
                if tokmajor:
                    add(ph, "dve", lambda e, sb=sb, bank=bank, b4=b4: e.tensor_tensor(
                        sb, bank[:, :], bcast[:, b4 * 512:(b4 + 1) * 512],
                        ALU.add).then_inc(VS, 1))
                else:
                    add(ph, "dve", lambda e, sb=sb, bank=bank, a=a: e.tensor_scalar(
                        sb, bank[:, :], bias_sb[:, a:a + 1], None,
                        ALU.add).then_inc(VS, 1))
                evac[idx] = pf_hist[bankk] = bump("VS")
                wait(ph, "sync", VS, cnt["VS"])
                if vhead:
                    def vst(e, sb=sb, dst=dst_ap, a=a, b4=b4, dst_off=dst_off):
                        dv = dst[:, dst_off:dst_off + 32768].rearrange(
                            "p (h a2 n) -> p h a2 n", h=16, a2=16)
                        e.dma_start(out=dv[:, b4 * 4:(b4 + 1) * 4, a, :],
                                    in_=sb.rearrange("p (h n) -> p h n", h=4)
                                    ).then_inc(SD, 16)
                    add(ph, "sync", vst)
                else:
                    off = dst_off + a * 2048 + b4 * 512
                    add(ph, "sync", lambda e, sb=sb, dst=dst_ap, off=off: e.dma_start(
                        out=dst[:, off: off + 512], in_=sb).then_inc(SD, 16))
                stage_dma[su % 2] = bump("SD", 16)
                stage_use[0] += 1

    proj_pass(ph, S3, bk_sb, None, cc2_in.ap(), 0, False, 0, gd_wk)
    k_ts = cnt["TS"]
    add(ph, "gpsimd", lambda e: e.wait_ge(TS, k_ts))
    add(ph, "gpsimd", lambda e: e.dma_start(
        out=S3[:, :].rearrange("p (c h) -> p c h", c=16),
        in_=wv_d.rearrange("(c p) h -> p c h", p=128)).then_inc(GD, 16))
    gd_wv = bump("GD", 16)
    proj_pass(ph, S3, None, None, cc2_in.ap(), 32768, True, 2, gd_wv, vhead=True)
    v_ts = cnt["TS"]
    add(ph, "gpsimd", lambda e: e.wait_ge(TS, v_ts))
    add(ph, "gpsimd", lambda e: e.dma_start(
        out=S3[:, :].rearrange("p (c h) -> p c h", c=16),
        in_=wq_d.rearrange("(c p) h -> p c h", p=128)).then_inc(GD, 16))
    gd_wq = bump("GD", 16)

    wait(ph, "sync", TS, v_ts)
    ln_pass(ph, xq_d)

    xq_sd = cnt["SD"]
    wait(ph, "gpsimd", SD, xq_sd)
    wait(ph, "gpsimd", GD, gd_wq)
    add(ph, "gpsimd", lambda e: e.collective_compute(
        "AllGather", ALU.bypass,
        replica_groups=[[0, 1], [2, 3], [4, 5], [6, 7]],
        ins=[cc2_in.ap().opt()], outs=[cc2_out.ap().opt()]).then_inc(CS, 1))
    cs_kv = bump("CS")

    S2q = S2[:, :].rearrange("p (h bl rr a) -> p h bl rr a", h=16, bl=2, rr=64, a=16)
    q_evac = {}
    for a in range(16):
        for b4 in range(4):
            idx = a * 4 + b4
            bankk = idx % 2
            bank = pf[bankk]
            if idx < 2:
                wait(ph, "pe", GD, gd_wq)
                wait(ph, "pe", SS, cnt["SS"])
                wait(ph, "pe", VS, pf_hist[bankk])
            else:
                wait(ph, "pe", VS, q_evac[idx - 2])

            def mmq(e, a=a, b4=b4, bank=bank):
                for dc in range(NDC):
                    inst = e.matmul(
                        bank[:, :],
                        S3[:, dc * 2048 + a * 128: dc * 2048 + (a + 1) * 128],
                        S1[:, dc * 2048 + b4 * 512: dc * 2048 + (b4 + 1) * 512],
                        start=(dc == 0), stop=(dc == NDC - 1))
                inst.then_inc(TS, 1)
            add(ph, "pe", mmq)
            bump("TS")
            wait(ph, "dve", TS, cnt["TS"])

            def qev(e, a=a, b4=b4, bank=bank):
                dst = S2q[:, b4 * 4:(b4 + 1) * 4, :, :, a]
                srcv = bank[:, :].rearrange("p (h bl rr) -> p h bl rr", h=4, bl=2)
                e.tensor_scalar(dst, srcv, bq_sb[:, a:a + 1], INV_SQRT_DH,
                                ALU.add, ALU.mult).then_inc(VS, 1)
            add(ph, "dve", qev)
            q_evac[idx] = pf_hist[bankk] = bump("VS")
    wait(ph, "sync", VS, cnt["VS"])
    wait(ph, "sync", CS, cs_kv)
    add(ph, "sync", lambda e: e.dma_start(out=qT_d, in_=S2[:, :]).then_inc(SD, 16))
    bump("SD", 16)
    p2_sd = cnt["SD"]

    ph = 3

    dec_used, dec_tp, flush_sd = {}, {}, {}
    for mi, (wp_d, ws_d) in enumerate([(gwp, gws), (uwp, uws), (dwp, dws)]):
        for g in range(16):
            b = g % 2
            prev_used = dec_used.get((mi, g - 2)) or (dec_used.get((mi - 1, 14 + g)) if mi > 0 else None)
            if prev_used:
                wait(ph, "sync", VS, prev_used)
            add(ph, "sync", lambda e, b=b, g=g, wp=wp_d: e.dma_start(
                out=pk_t[b], in_=wp[g * 128:(g + 1) * 128, :]).then_inc(SD, 16))
            add(ph, "sync", lambda e, b=b, g=g, ws=ws_d: e.dma_start(
                out=dsc[b], in_=ws[g * 128:(g + 1) * 128, :]).then_inc(SD, 16))
            bump("SD", 32)
            wait(ph, "dve", SD, cnt["SD"])
            prev_tp = dec_tp.get((mi, g - 2)) or (dec_tp.get((mi - 1, 14 + g)) if mi > 0 else None)
            if prev_tp:
                wait(ph, "dve", TS, prev_tp)

            v0 = cnt["VS"]
            def dec(e, b=b, v0=v0):
                t0, t1, t2, t3, t4, t5 = dec_t
                w = pk_t[b]
                v = v0
                def st(inst):
                    nonlocal v
                    v += 1
                    inst.then_inc(VS, 1)
                    e.wait_ge(VS, v)
                st(e.tensor_scalar(t0, w, 7, None, ALU.bitwise_and))
                st(e.tensor_scalar(t1, w, 4, 7, ALU.logical_shift_right, ALU.bitwise_and))
                st(e.tensor_scalar(t2, t0, 64, 16128, ALU.mult, ALU.add))
                st(e.scalar_tensor_tensor(t3, t0, 16128.0, t2, ALU.mult, ALU.min))
                st(e.tensor_scalar(t4, t1, 64, 16128, ALU.mult, ALU.add))
                st(e.scalar_tensor_tensor(t5, t1, 16128.0, t4, ALU.mult, ALU.min))
                st(e.tensor_scalar(t0, w, 8, 12, ALU.bitwise_and, ALU.logical_shift_left))
                st(e.tensor_scalar(t1, w, 128, 8, ALU.bitwise_and, ALU.logical_shift_left))
                st(e.tensor_tensor(t2, t3, t0, ALU.add))
                st(e.tensor_tensor(t4, t5, t1, ALU.add))
                dvv = decb[b].rearrange("p (a c) -> p a c", c=32)
                t2b = t2.bitcast(BF16).rearrange("p (a c) -> p a c", c=32)
                t4b = t4.bitcast(BF16).rearrange("p (a c) -> p a c", c=32)
                scb = dsc[b].unsqueeze(2).broadcast_to([128, 64, 16])
                st(e.scalar_tensor_tensor(dvv[:, :, 0:32:2], t2b[:, :, 0:32:2], 1.0,
                                          scb, ALU.mult, ALU.mult))
                st(e.scalar_tensor_tensor(dvv[:, :, 1:32:2], t4b[:, :, 0:32:2], 1.0,
                                          scb, ALU.mult, ALU.mult))
            add(ph, "dve", dec)
            dec_used[(mi, g)] = bump("VS", 12)
            vs_dec = cnt["VS"]
            for half in range(2):
                bi = half
                wait(ph, "pe", VS, vs_dec)
                wait(ph, "pe", SS, pb_free[bi])
                if mi > 0 and g == 0 and half == 0:
                    wait(ph, "pe", SD, flush_sd[mi - 1])

                def tp(e, b=b, bank=pb[bi], half=half):
                    for i in range(8):
                        dc = half * 8 + i
                        inst = e.transpose(bank[:, i * 128:(i + 1) * 128],
                                           decb[b][:, dc * 128:(dc + 1) * 128], ident)
                    inst.then_inc(TS, 1)
                add(ph, "pe", tp)
                bump("TS")
                wait(ph, "act", TS, cnt["TS"])

                def ev(e, bank=pb[bi], half=half, g=g):
                    for i in range(8):
                        dc = half * 8 + i
                        inst = e.copy(S3[:, dc * 2048 + g * 128: dc * 2048 + g * 128 + 128],
                                      bank[:, i * 128:(i + 1) * 128])
                    inst.then_inc(SS, 1)
                add(ph, "act", ev)
                pb_free[bi] = bump("SS")
            dec_tp[(mi, g)] = cnt["TS"]
        wait(ph, "sync", SS, cnt["SS"])
        add(ph, "sync", lambda e, mi=mi: e.dma_start(
            out=cc1_in[mi].ap(), in_=S3[:, :]).then_inc(SD, 16))
        flush_sd[mi] = bump("SD", 16)
    wait(ph, "gpsimd", SD, flush_sd[2])
    add(ph, "gpsimd", lambda e: e.dma_start(
        out=S3[:, :].rearrange("p (c h) -> p c h", c=16), in_=wo_d.rearrange("(c p) h -> p c h", p=128)).then_inc(GD, 16))
    gd_wo = bump("GD", 16)
    p3_sd = cnt["SD"]

    ph = 4
    for eng in ("pe", "act", "dve", "sync"):
        wait(ph, eng, CS, cs_kv)
    wait(ph, "sync", SD, p3_sd)
    wait(ph, "gpsimd", CS, cs_kv)
    wait(ph, "gpsimd", SD, flush_sd[2])
    for mi in range(3):
        add(ph, "gpsimd", lambda e, mi=mi: e.collective_compute(
            "AllGather", ALU.bypass, replica_groups=[list(range(R))],
            ins=[cc1_in[mi].ap().opt()], outs=[cc1_out[mi].ap().opt()]).then_inc(CS, 1))
        bump("CS")
    cs_w = cnt["CS"]

    att_done, att_evac, att_binv, att_edma, att_recip = {}, {}, {}, {}, {}

    def prev_key(s, h, lag):
        if h - lag >= 0:
            return (s, h - lag)
        if s > 0:
            return (s - 1, 16 + h - lag)
        return None

    for s in range(4):
        lt = SLOT_LT[s]
        kmax = SLOT_KMAX[s]
        mstart = SLOT_MSTART[s]
        runs = kv_runs(kmax)

        def gen_masks(e, s=s, kmax=kmax, mstart=mstart):
            for jm in range(mstart, kmax):
                inst = e.tensor_scalar(
                    masks[:, (jm - mstart) * 512:(jm - mstart + 1) * 512],
                    colq, mrow[:, s * 32 + jm: s * 32 + jm + 1], None, ALU.is_ge)
            inst.then_inc(VS, 1)
        add(ph, "dve", gen_masks)
        mask_vs = bump("VS")

        for h in range(NH):
            bi = h % 2
            kc, vc, qc = att_bufs[bi]
            pk2 = prev_key(s, h, 2)
            pk1 = prev_key(s, h, 1)
            if pk2 and pk2 in att_done:
                wait(ph, "sync", TS, att_done[pk2])
                wait(ph, "gpsimd", TS, att_done[pk2])
            for (j0, r, tc0, ln) in runs:
                add(ph, "sync", lambda e, j0=j0, r=r, tc0=tc0, ln=ln, kc=kc, h=h: e.dma_start(
                    out=kc[:, j0 * 128:(j0 + ln) * 128],
                    in_=cc2_out.ap()[r * 128:(r + 1) * 128,
                                     h * 2048 + tc0 * 128: h * 2048 + (tc0 + ln) * 128]
                ).then_inc(SD, 16))
                bump("SD", 16)

                def vload(e, j0=j0, r=r, tc0=tc0, ln=ln, vc=vc, h=h):
                    e.dma_start(
                        out=vc[:, j0 * 128:(j0 + ln) * 128],
                        in_=cc2_out.ap()[r * 128:(r + 1) * 128,
                                         32768 + h * 2048 + tc0 * 128:
                                         32768 + h * 2048 + (tc0 + ln) * 128]
                    ).then_inc(GD, 16)
                add(ph, "gpsimd", vload)
                bump("GD", 16)
            dma_tgt = cnt["SD"]
            add(ph, "gpsimd", lambda e, qc=qc, h=h, lt=lt: e.dma_start(
                out=qc, in_=qT_d[:, h * 2048 + lt * 512: h * 2048 + lt * 512 + 512]
            ).then_inc(GD, 16))
            dma_tgt_gd = bump("GD", 16)

            av_bank = pf[3 + bi]
            sm_bank = pf[5]
            exp_ss = {}
            av_ts = {}
            pt_of = {}

            def emit_st(j):
                sbank = pf[j % 3]
                if j == 0:
                    wait(ph, "pe", SD, dma_tgt)
                    wait(ph, "pe", GD, dma_tgt_gd)
                    wait(ph, "pe", VS, mask_vs)
                    pe_evac = att_evac.get(pk2) if pk2 else None
                    if pe_evac:
                        wait(ph, "pe", VE, pe_evac)
                    if pk1 and pk1 in att_recip:
                        wait(ph, "pe", SS, att_recip[pk1])
                if j == 2 and pk1 and pk1 in att_binv:
                    wait(ph, "pe", VE, att_binv[pk1])
                if j >= 3:
                    wait(ph, "pe", SS, exp_ss[j - 3])
                add(ph, "pe", lambda e, sbank=sbank, kc=kc, qc=qc, j=j: e.matmul(
                    sbank[:, :], kc[:, j * 128:(j + 1) * 128], qc,
                    start=True, stop=True).then_inc(TS, 1))
                st_ts = bump("TS")
                wait(ph, "act", TS, st_ts)
                if j >= 3:
                    wait(ph, "act", TS, av_ts[j - 3])
                elif pk1 and pk1 in att_done:
                    wait(ph, "act", TS, att_done[pk1])
                masked = j >= mstart
                add(ph, "act", lambda e, j=j, sbank=sbank: e.activation(
                    ptile[j % 3], sbank[:, :], AF.Exp).then_inc(SS, 1))
                exp_ss[j] = bump("SS")
                if masked:
                    wait(ph, "dve", SS, exp_ss[j])
                    wait(ph, "dve", VS, mask_vs)
                    add(ph, "dve", lambda e, j=j, mstart=mstart: e.tensor_tensor(
                        ptile[j % 3], ptile[j % 3],
                        masks[:, (j - mstart) * 512:(j - mstart + 1) * 512],
                        ALU.mult).then_inc(PT, 1))
                    pt_of[j] = ("PT", bump("PT"))
                else:
                    pt_of[j] = ("SS", exp_ss[j])

            def emit_av(j):
                kind, val = pt_of[j]
                wait(ph, "pe", PT if kind == "PT" else SS, val)

                def av(e, j=j, vc=vc, av_bank=av_bank, sm_bank=sm_bank, kmax=kmax):
                    e.matmul(av_bank[:, :], vc[:, j * 128:(j + 1) * 128], ptile[j % 3],
                             start=(j == 0), stop=(j == kmax - 1))
                    e.matmul(sm_bank[0:1, :], ones_col, ptile[j % 3],
                             start=(j == 0), stop=(j == kmax - 1)).then_inc(TS, 1)
                add(ph, "pe", av)
                av_ts[j] = bump("TS")

            for j in range(kmax):
                emit_st(j)
                if j >= 2:
                    emit_av(j - 2)
            emit_av(kmax - 2)
            emit_av(kmax - 1)
            att_done[(s, h)] = cnt["TS"]
            wait(ph, "act", TS, cnt["TS"])
            s0 = cnt["SS"]
            def lninv(e, sm_bank=sm_bank, s0=s0):
                e.activation(lntmp, sm_bank[0:1, :], AF.Ln).then_inc(SS, 1)
                e.wait_ge(SS, s0 + 1)
                e.activation(inv_sb, lntmp, AF.Exp, scale=-1.0).then_inc(SS, 1)
            add(ph, "act", lninv)
            bump("SS")
            att_recip[(s, h)] = bump("SS")
            wait(ph, "pe", SS, cnt["SS"])
            add(ph, "pe", lambda e: e.matmul(pf[2][:, :], ones_row, inv_sb,
                                             start=True, stop=True).then_inc(TS, 1))
            bump("TS")
            wait(ph, "dve", TS, cnt["TS"])
            add(ph, "dve", lambda e: e.tensor_copy(invb, pf[2][:, :]).then_inc(VE, 1))
            att_binv[(s, h)] = bump("VE")
            wait(ph, "dve", VE, att_binv[(s, h)])
            if pk2 and pk2 in att_edma:
                wait(ph, "dve", SD, att_edma[pk2])
            add(ph, "dve", lambda e, bi=bi, av_bank=av_bank: e.tensor_tensor(
                evT[bi], av_bank[:, :], invb, ALU.mult).then_inc(VE, 1))
            att_evac[(s, h)] = bump("VE")
            wait(ph, "sync", VE, cnt["VE"])
            add(ph, "sync", lambda e, bi=bi, h=h, lt=lt: e.dma_start(
                out=attnT_d[:, h * 2048 + lt * 512: h * 2048 + lt * 512 + 512],
                in_=evT[bi]).then_inc(SD, 16))
            att_edma[(s, h)] = bump("SD", 16)
    p4_sd = cnt["SD"]

    ph = 5
    wait(ph, "sync", SD, p4_sd)
    add(ph, "gpsimd", lambda e: e.dma_start(out=browb, in_=bo_d).then_inc(GD, 16))
    gd_bo = bump("GD", 16)
    p5_cp = {}
    for i in range(4):
        wait(ph, "pe", GD, gd_bo)
        wait(ph, "pe", GD, gd_wo)
        if i >= 2:
            wait(ph, "pe", VS, p5_cp[i - 2])
        else:
            wait(ph, "pe", VS, pf_hist[i])
        add(ph, "pe", lambda e, i=i: e.matmul(
            pf[i % 2][:, :], ones_rowb, browb[0:1, i * 512:(i + 1) * 512],
            start=True, stop=True).then_inc(TS, 1))
        bump("TS")
        wait(ph, "dve", TS, cnt["TS"])
        add(ph, "dve", lambda e, i=i: e.tensor_copy(
            bcast[:, i * 512:(i + 1) * 512], pf[i % 2][:, :]).then_inc(VS, 1))
        p5_cp[i] = pf_hist[i] = bump("VS")
    bo_vs = cnt["VS"]

    p5_mm, p5_ev, p5_ln3, p5_x1sd, p5_tp = {}, {}, {}, {}, {}
    for tt in range(16):
        b = tt % 2
        at = p5_at[b]
        if tt >= 2:
            wait(ph, "sync", TS, p5_mm[tt - 2])
        if tt >= 1:
            wait(ph, "sync", VS, p5_ln3[tt - 1])
            wait(ph, "sync", SD, p5_x1sd[tt - 1])

        def at_load(e, at=at, tt=tt):
            src = attnT_d.rearrange("p (h t) -> p h t", h=NH)[:, :, tt * 128:(tt + 1) * 128]
            e.dma_start(out=at.rearrange("p (h n) -> p h n", h=NH), in_=src).then_inc(SD, 16)
        add(ph, "sync", at_load)
        add(ph, "sync", lambda e, tt=tt: e.dma_start(
            out=p5_x, in_=xo[tt * 128:(tt + 1) * 128, :]).then_inc(SD, 16))
        x_sd = bump("SD", 32)
        for dt in range(4):
            idx = tt * 4 + dt
            bankk = 2 + idx % 2
            bank = pf[bankk]
            if idx == 0:
                wait(ph, "pe", VS, bo_vs)
            wait(ph, "pe", SD, x_sd)
            if idx >= 2:
                wait(ph, "pe", VS, p5_ev[idx - 2])
            else:
                wait(ph, "pe", VS, pf_hist[bankk])

            def mm(e, at=at, bank=bank, dt=dt):
                for hc in range(NH):
                    inst = e.matmul(bank[:, :], at[:, hc * 128:(hc + 1) * 128],
                                    S3[:, hc * 2048 + dt * 512: hc * 2048 + (dt + 1) * 512],
                                    start=(hc == 0), stop=(hc == NH - 1))
                inst.then_inc(TS, 1)
            add(ph, "pe", mm)
            bump("TS")
            wait(ph, "dve", TS, cnt["TS"])
            if dt == 0 and tt >= 1:
                wait(ph, "dve", VS, p5_ln3[tt - 1])
            add(ph, "dve", lambda e, bank=bank, dt=dt: e.tensor_tensor(
                p5_y[:, dt * 512:(dt + 1) * 512], bank[:, :],
                bcast[:, dt * 512:(dt + 1) * 512], ALU.add).then_inc(VS, 1))
            p5_ev[idx] = pf_hist[bankk] = bump("VS")
        p5_mm[tt] = cnt["TS"]
        wait(ph, "dve", VS, p5_ev[tt * 4 + 3])
        ln_stats(ph, p5_y)

        v0 = cnt["VS"]
        def ln2_apply(e, v0=v0):
            e.tensor_scalar(p5_y, p5_y, mv[:, 0:1], None, ALU.subtract).then_inc(VS, 1)
            e.wait_ge(VS, v0 + 1)
            e.tensor_scalar(p5_y, p5_y, rstd, None, ALU.mult).then_inc(VS, 1)
            e.wait_ge(VS, v0 + 2)
            e.tensor_tensor(p5_x, p5_x, p5_y, ALU.add).then_inc(VS, 1)
        add(ph, "dve", ln2_apply)
        bump("VS", 3)
        wait(ph, "sync", VS, cnt["VS"])
        add(ph, "sync", lambda e, tt=tt: e.dma_start(
            out=x1_d[tt * 128:(tt + 1) * 128, :], in_=p5_x).then_inc(SD, 16))
        p5_x1sd[tt] = bump("SD", 16)
        ln_stats(ph, p5_x)
        if tt >= 1:
            wait(ph, "dve", TS, p5_tp[tt - 1])

        v0 = cnt["VS"]
        def ln3_apply(e, v0=v0):
            e.tensor_scalar(p5_y, p5_x, mv[:, 0:1], None, ALU.subtract).then_inc(VS, 1)
            e.wait_ge(VS, v0 + 1)
            e.tensor_scalar(p5_xn, p5_y, rstd, None, ALU.mult).then_inc(VS, 1)
        add(ph, "dve", ln3_apply)
        bump("VS")
        p5_ln3[tt] = bump("VS")
        wait(ph, "pe", VS, cnt["VS"])
        p5_tp[tt] = transposes(ph, p5_xn, S1, tt)
    p5_sd = cnt["SD"]

    ph = 6
    wait(ph, "sync", CS, cs_w)
    wait(ph, "sync", SD, p5_sd)

    gu_ts = {}
    inter_vs = {}
    t3_vs = {}
    wdma = {}
    down_ts = {}
    down_hist = {4: pf_hist[4], 5: pf_hist[5]}
    flush_slice = {}
    out_add_vs = {}
    out_so = {}
    n_out = 0

    def emit_down_group(to, g8):
        dg = to * 16 + g8
        last_gidx = to * 128 + g8 * 8 + 7
        for ts4 in range(4):
            for ht in range(4):
                q = ts4 * 4 + ht
                bankk = 4 + q % 2
                bank = pf[bankk]
                if q == 0:
                    wait(ph, "pe", VS, inter_vs[last_gidx])
                    wait(ph, "pe", SD, wd_sd[dg])
                wait(ph, "pe", VS, down_hist[bankk])

                def dmm(e, ts4=ts4, ht=ht, bank=bank, g8=g8):
                    for i8 in range(8):
                        sl = (g8 * 8 + i8) % 16
                        inst = e.matmul(
                            bank[:, :],
                            it_ring[sl][:, ts4 * 128:(ts4 + 1) * 128],
                            wd_ring[i8][:, ht * 512:(ht + 1) * 512],
                            start=(i8 == 0), stop=(i8 == 7))
                    inst.then_inc(TS, 1)
                add(ph, "pe", dmm)
                bump("TS")
                wait(ph, "dve", TS, cnt["TS"])
                pfl = flush_slice.get((ts4, ht))
                if pfl:
                    wait(ph, "dve", VS, pfl)
                if g8 == 0:
                    add(ph, "dve", lambda e, bank=bank, ts4=ts4, ht=ht: e.tensor_copy(
                        acc[:, ts4 * 2048 + ht * 512: ts4 * 2048 + (ht + 1) * 512],
                        bank[:, :]).then_inc(VS, 1))
                else:
                    add(ph, "dve", lambda e, bank=bank, ts4=ts4, ht=ht: e.tensor_tensor(
                        acc[:, ts4 * 2048 + ht * 512: ts4 * 2048 + (ht + 1) * 512],
                        acc[:, ts4 * 2048 + ht * 512: ts4 * 2048 + (ht + 1) * 512],
                        bank[:, :], ALU.add).then_inc(VS, 1))
                down_hist[bankk] = flush_slice[(ts4, ht)] = bump("VS")
        down_ts[dg] = cnt["TS"]

    def emit_wd_load(to, c):
        g_of_c = c // 8
        dgp = to * 16 + g_of_c - 1
        if dgp >= 0 and dgp in down_ts:
            wait(ph, "sync", TS, down_ts[dgp])
        r, o = c // 16, c % 16
        add(ph, "sync", lambda e, r=r, o=o, ring_i=c % 8: e.dma_start(
            out=wd_ring[ring_i],
            in_=cc1_out[2].ap()[r * 128:(r + 1) * 128,
                                o * 2048:(o + 1) * 2048]).then_inc(SD, 16))
        wd_sd[to * 16 + g_of_c] = bump("SD", 16)

    wd_sd = {}
    for to in range(4):
        for nci in range(128):
            r, o = nci // 16, nci % 16
            buf = nci % 2
            ring_i = nci % 16
            g8 = nci // 8
            gidx = to * 128 + nci
            if gidx >= 2:
                wait(ph, "sync", TS, gu_ts[gidx - 2])

            def wload(e, r=r, o=o, buf=buf):
                src_g = cc1_out[0].ap()[r * 128:(r + 1) * 128, :]
                src_g = src_g.rearrange("p (c n) -> p c n", c=16)[:, :, o * 128:(o + 1) * 128]
                e.dma_start(out=wg_t[buf].rearrange("p (c n) -> p c n", c=16),
                            in_=src_g).then_inc(SD, 16)
                src_u = cc1_out[1].ap()[r * 128:(r + 1) * 128, :]
                src_u = src_u.rearrange("p (c n) -> p c n", c=16)[:, :, o * 128:(o + 1) * 128]
                e.dma_start(out=wu_t[buf].rearrange("p (c n) -> p c n", c=16),
                            in_=src_u).then_inc(SD, 16)
            add(ph, "sync", wload)
            wdma[gidx] = bump("SD", 32)
            if nci >= 8:
                emit_wd_load(to, nci - 8)

            wait(ph, "pe", SD, wdma[gidx])
            if gidx >= 2:
                wait(ph, "pe", VS, inter_vs[gidx - 2])
            else:
                wait(ph, "pe", VS, max(pf_hist[buf], pf_hist[2 + buf]))

            def gmm(e, buf=buf, to=to):
                for c in range(NDC):
                    inst = e.matmul(pf[buf][:, :], wg_t[buf][:, c * 128:(c + 1) * 128],
                                    S1[:, c * 2048 + to * 512: c * 2048 + (to + 1) * 512],
                                    start=(c == 0), stop=(c == NDC - 1))
                inst.then_inc(TS, 1)
            add(ph, "pe", gmm)
            g_ts = bump("TS")

            def umm(e, buf=buf, to=to):
                for c in range(NDC):
                    inst = e.matmul(pf[2 + buf][:, :], wu_t[buf][:, c * 128:(c + 1) * 128],
                                    S1[:, c * 2048 + to * 512: c * 2048 + (to + 1) * 512],
                                    start=(c == 0), stop=(c == NDC - 1))
                inst.then_inc(TS, 1)
            add(ph, "pe", umm)
            gu_ts[gidx] = bump("TS")

            if nci % 8 == 7 and g8 >= 1:
                emit_down_group(to, g8 - 1)

            wait(ph, "act", TS, g_ts)
            if gidx >= 1:
                wait(ph, "act", VS, inter_vs[gidx - 1])
            add(ph, "act", lambda e, buf=buf: e.activation(
                g_sq, pf[buf][:, :], AF.Square).then_inc(SS, 1))
            sq_ss = bump("SS")
            wait(ph, "dve", SS, sq_ss)
            v0 = cnt["VS"]
            add(ph, "dve", lambda e, v0=v0: (
                e.tensor_scalar(g_t2, g_sq, 0.044715 * 0.7978845608028654,
                                0.7978845608028654, ALU.mult, ALU.add).then_inc(VS, 1),
                e.wait_ge(VS, v0 + 1)))
            bump("VS")
            add(ph, "dve", lambda e, buf=buf: e.tensor_tensor(
                g_t2, g_t2, pf[buf][:, :], ALU.mult).then_inc(VS, 1))
            t3_vs[gidx] = bump("VS")
            wait(ph, "act", VS, t3_vs[gidx])
            add(ph, "act", lambda e: e.activation(
                g_th, g_t2, AF.Tanh).then_inc(SS, 1))
            th_ss = bump("SS")
            wait(ph, "dve", SS, th_ss)
            v0 = cnt["VS"]
            add(ph, "dve", lambda e, buf=buf, v0=v0: (
                e.scalar_tensor_tensor(g_b, g_th, 1.0, pf[buf][:, :],
                                       ALU.add, ALU.mult).then_inc(VS, 1),
                e.wait_ge(VS, v0 + 1)))
            bump("VS")
            wait(ph, "dve", TS, gu_ts[gidx])
            add(ph, "dve", lambda e, buf=buf, ring_i=ring_i: e.scalar_tensor_tensor(
                it_ring[ring_i], g_b, 0.5, pf[2 + buf][:, :],
                ALU.mult, ALU.mult).then_inc(VS, 1))
            inter_vs[gidx] = pf_hist[buf] = pf_hist[2 + buf] = bump("VS")

        for c in range(120, 128):
            emit_wd_load(to, c)
        emit_down_group(to, 15)

        for ts4 in range(4):
            acc_sl = acc[:, ts4 * 2048:(ts4 + 1) * 2048]
            pk = out_add_vs.get((to - 1, ts4))
            if pk:
                wait(ph, "gpsimd", VS, pk)
            add(ph, "gpsimd", lambda e, to=to, ts4=ts4: e.dma_start(
                out=p6_x1, in_=x1_d[to * 512 + ts4 * 128: to * 512 + (ts4 + 1) * 128, :]
            ).then_inc(GD, 16))
            x1_sd = bump("GD", 16)
            wait(ph, "dve", VS, flush_slice[(ts4, 3)])
            ln_stats(ph, acc_sl)

            v0 = cnt["VS"]
            def ln4_apply(e, acc_sl=acc_sl, v0=v0):
                e.tensor_scalar(acc_sl, acc_sl, mv[:, 0:1], None,
                                ALU.subtract).then_inc(VS, 1)
                e.wait_ge(VS, v0 + 1)
                e.tensor_scalar(acc_sl, acc_sl, rstd, None, ALU.mult).then_inc(VS, 1)
            add(ph, "dve", ln4_apply)
            bump("VS", 2)
            wait(ph, "dve", GD, x1_sd)
            if n_out >= 1:
                wait(ph, "dve", SO, out_so[n_out - 1])
            add(ph, "dve", lambda e, acc_sl=acc_sl: e.tensor_tensor(
                p6_o, acc_sl, p6_x1, ALU.add).then_inc(VS, 1))
            out_add_vs[(to, ts4)] = bump("VS")
            wait(ph, "gpsimd", VS, cnt["VS"])
            add(ph, "gpsimd", lambda e, to=to, ts4=ts4: e.dma_start(
                out=out_d[to * 512 + ts4 * 128: to * 512 + (ts4 + 1) * 128, :],
                in_=p6_o).then_inc(SO, 16))
            out_so[n_out] = bump("SO", 16)
            n_out += 1

    wait(ph, "sync", SO, cnt["SO"])
    if dbg:
        add(ph, "sync", lambda e: e.dma_start(out=d_qT, in_=qT_d).then_inc(SO, 16))
        add(ph, "sync", lambda e: e.dma_start(out=d_kv, in_=cc2_out.ap()).then_inc(SO, 16))
        add(ph, "sync", lambda e: e.dma_start(out=d_att, in_=attnT_d).then_inc(SO, 16))
        add(ph, "sync", lambda e: e.dma_start(out=d_x1, in_=x1_d).then_inc(SO, 16))
        add(ph, "sync", lambda e: e.dma_start(
            out=d_w, in_=cc1_out.ap()[:, 0:2048]).then_inc(SO, 16))
        add(ph, "sync", lambda e: e.dma_start(
            out=d_wd, in_=cc1_out.ap()[:, 65536:67584]).then_inc(SO, 16))
        bump("SO", 96)
        wait(ph, "sync", SO, cnt["SO"])

    def runner(lst):
        return lambda e: [f(e) for f in lst]

    with es:
        for p in range(NPH):
            with nc.Block() as blk:
                blk.tensor(runner(streams[p]["pe"]))
                blk.vector(runner(streams[p]["dve"]))
                blk.scalar(runner(streams[p]["act"]))
                blk.gpsimd(runner(streams[p]["gpsimd"]))
                blk.sync(runner(streams[p]["sync"]))
    return nc


_NC_CACHE = [None]
DBG = [False]
_LAST_RES = [None]


def kernel(L0_attn_x, L0_attn_Wq, L0_attn_bq, L0_attn_Wk, L0_attn_bk,
           L0_attn_Wv, L0_attn_bv, L0_attn_Wo, L0_attn_bo,
           L0_ffn_gate_W_packed, L0_ffn_gate_scales,
           L0_ffn_up_W_packed, L0_ffn_up_scales,
           L0_ffn_down_W_packed, L0_ffn_down_scales):
    x = np.ascontiguousarray(np.asarray(L0_attn_x, dtype=np.float32))
    wq = np.ascontiguousarray(np.asarray(L0_attn_Wq, dtype=np.float32))
    wk = np.ascontiguousarray(np.asarray(L0_attn_Wk, dtype=np.float32))
    wv = np.ascontiguousarray(np.asarray(L0_attn_Wv, dtype=np.float32))
    wo = np.ascontiguousarray(np.asarray(L0_attn_Wo, dtype=np.float32))
    bq = np.asarray(L0_attn_bq, dtype=np.float32)
    bk = np.asarray(L0_attn_bk, dtype=np.float32)
    bv = np.asarray(L0_attn_bv, dtype=np.float32)
    bo = np.asarray(L0_attn_bo, dtype=np.float32)
    gp = np.asarray(L0_ffn_gate_W_packed, dtype=np.int32)
    gs = np.asarray(L0_ffn_gate_scales, dtype=np.float32)
    up = np.asarray(L0_ffn_up_W_packed, dtype=np.int32)
    us = np.asarray(L0_ffn_up_scales, dtype=np.float32)
    dp = np.asarray(L0_ffn_down_W_packed, dtype=np.int32)
    ds = np.asarray(L0_ffn_down_scales, dtype=np.float32)

    bq_r = np.ascontiguousarray(bq.reshape(NDC, 128).T)
    bk_r = np.ascontiguousarray(bk.reshape(NDC, 128).T)
    bv_row = np.ascontiguousarray(bv.reshape(1, D))
    bo_row = np.ascontiguousarray(bo.reshape(1, D))

    in_maps = []
    for c in range(R):
        b = c // 2
        half = c % 2
        blocks = [0, 3] if half == 0 else [1, 2]
        x_own = np.concatenate([x[b, blk * 1024:(blk + 1) * 1024, :] for blk in blocks],
                               axis=0)
        qrows = []
        for h in range(NH):
            for blk in blocks:
                qrows.extend(range(h * 256 + blk * 64, h * 256 + blk * 64 + 64))
        xq_own = x[b][np.asarray(qrows)]
        maskrow = np.zeros((128, 128), dtype=np.float32)
        si = np.arange(128, dtype=np.float32)[:, None]
        for s in range(4):
            lt = SLOT_LT[s]
            qg0 = QG0[half][lt]
            for j in range(32):
                maskrow[:, s * 32 + j: s * 32 + j + 1] = si + (j * 128 - qg0)
        in_maps.append({
            "xo": np.ascontiguousarray(x_own),
            "xq": np.ascontiguousarray(xq_own),
            "wq": wq, "wk": wk, "wv": wv, "wo": wo,
            "bq_r": bq_r, "bk_r": bk_r, "bv_row": bv_row, "bo_row": bo_row,
            "gwp": np.ascontiguousarray(gp[c * 2048:(c + 1) * 2048, :]),
            "gws": np.ascontiguousarray(gs[c * 2048:(c + 1) * 2048, :]),
            "uwp": np.ascontiguousarray(up[c * 2048:(c + 1) * 2048, :]),
            "uws": np.ascontiguousarray(us[c * 2048:(c + 1) * 2048, :]),
            "dwp": np.ascontiguousarray(dp[:, c * 1024:(c + 1) * 1024]),
            "dws": np.ascontiguousarray(ds[:, c * 64:(c + 1) * 64]),
            "maskrow": maskrow,
        })

    if _NC_CACHE[0] is None:
        _NC_CACHE[0] = build_program(dbg=DBG[0])
    nc = _NC_CACHE[0]
    res = run_bass_kernel_spmd(nc, in_maps, core_ids=list(range(R)))
    if DBG[0]:
        _LAST_RES[0] = res

    out = np.empty((B_, T_, D), dtype=np.float32)
    for c in range(R):
        b = c // 2
        half = c % 2
        blocks = [0, 3] if half == 0 else [1, 2]
        o = np.asarray(res.results[c]["out"], dtype=np.float32)
        for i, blk in enumerate(blocks):
            out[b, blk * 1024:(blk + 1) * 1024, :] = o[i * 1024:(i + 1) * 1024, :]
    return out


# version 19
# speedup vs baseline: 1.0693x; 1.0016x over previous
import contextlib
from collections import defaultdict

import numpy as np

import concourse.bass as bass
import concourse.mybir as mybir
from concourse.bass_utils import run_bass_kernel_spmd

F32 = mybir.dt.float32
BF16 = mybir.dt.bfloat16
I32 = mybir.dt.int32
U16 = mybir.dt.uint16
AF = mybir.ActivationFunctionType
ALU = mybir.AluOpType

R = 8
B_, T_, D = 4, 4096, 2048
NH = 16
TOK = 2048
NDC = 16
INV_SQRT_DH = float(np.float32(1.0) / np.float32(np.sqrt(np.float32(128.0))))
EPS = 1e-5

SLOT_LT = [3, 2, 1, 0]
SLOT_KMAX = [32, 28, 16, 12]
SLOT_MSTART = [20, 16, 4, 0]
N_MASK = 12
QG0 = {0: [0, 512, 3072, 3584], 1: [1024, 1536, 2048, 2560]}


def chunk_map(j):
    blk = j // 8
    r = [0, 1, 1, 0][blk]
    tc = (j % 8) + (8 if blk >= 2 else 0)
    return r, tc


def kv_runs(kmax):
    runs, j = [], 0
    while j < kmax:
        r, tc = chunk_map(j)
        ln = 1
        while j + ln < kmax:
            r2, tc2 = chunk_map(j + ln)
            if r2 == r and tc2 == tc + ln:
                ln += 1
            else:
                break
        runs.append((j, r, tc, ln))
        j += ln
    return runs


def build_program(dbg=False):
    nc = bass.Bass("TRN2", target_bir_lowering=False)

    xo = nc.dram_tensor("xo", [TOK, D], F32, kind="ExternalInput").ap()
    xq_d = nc.dram_tensor("xq", [TOK, D], F32, kind="ExternalInput").ap()
    wq_d = nc.dram_tensor("wq", [D, D], F32, kind="ExternalInput").ap()
    wk_d = nc.dram_tensor("wk", [D, D], F32, kind="ExternalInput").ap()
    wv_d = nc.dram_tensor("wv", [D, D], F32, kind="ExternalInput").ap()
    wo_d = nc.dram_tensor("wo", [D, D], F32, kind="ExternalInput").ap()
    bq_d = nc.dram_tensor("bq_r", [128, NDC], F32, kind="ExternalInput").ap()
    bk_d = nc.dram_tensor("bk_r", [128, NDC], F32, kind="ExternalInput").ap()
    bv_d = nc.dram_tensor("bv_row", [1, D], F32, kind="ExternalInput").ap()
    bo_d = nc.dram_tensor("bo_row", [1, D], F32, kind="ExternalInput").ap()
    gwp = nc.dram_tensor("gwp", [2048, 1024], I32, kind="ExternalInput").ap()
    gws = nc.dram_tensor("gws", [2048, 64], F32, kind="ExternalInput").ap()
    uwp = nc.dram_tensor("uwp", [2048, 1024], I32, kind="ExternalInput").ap()
    uws = nc.dram_tensor("uws", [2048, 64], F32, kind="ExternalInput").ap()
    dwp = nc.dram_tensor("dwp", [2048, 1024], I32, kind="ExternalInput").ap()
    dws = nc.dram_tensor("dws", [2048, 64], F32, kind="ExternalInput").ap()
    mrow_d = nc.dram_tensor("maskrow", [128, 128], F32, kind="ExternalInput").ap()
    out_d = nc.dram_tensor("out", [TOK, D], F32, kind="ExternalOutput").ap()

    qT_d = nc.dram_tensor("qT_d", [128, NH * TOK], BF16).ap()
    attnT_d = nc.dram_tensor("attnT_d", [128, NH * TOK], BF16).ap()
    x1_d = nc.dram_tensor("x1_d", [TOK, D], F32).ap()
    cc2_in = nc.dram_tensor("cc2_in", [128, 2 * 32768], BF16)
    cc2_out = nc.dram_tensor("cc2_out", [256, 2 * 32768], BF16)
    cc1_in = [nc.dram_tensor(f"cc1_in{m}", [128, 32768], BF16) for m in range(3)]
    cc1_out = [nc.dram_tensor(f"cc1_out{m}", [128 * R, 32768], BF16, addr_space="Shared")
               for m in range(3)]
    if dbg:
        d_qT = nc.dram_tensor("d_qT", [128, NH * TOK], BF16, kind="ExternalOutput").ap()
        d_kv = nc.dram_tensor("d_kv", [256, 2 * 32768], BF16, kind="ExternalOutput").ap()
        d_att = nc.dram_tensor("d_att", [128, NH * TOK], BF16, kind="ExternalOutput").ap()
        d_x1 = nc.dram_tensor("d_x1", [TOK, D], F32, kind="ExternalOutput").ap()
        d_w = nc.dram_tensor("d_w", [1024, 2048], BF16, kind="ExternalOutput").ap()
        d_wd = nc.dram_tensor("d_wd", [1024, 2048], BF16, kind="ExternalOutput").ap()

    es = contextlib.ExitStack()
    S1 = es.enter_context(nc.sbuf_tensor("S1", [128, 32768], BF16))
    S2 = es.enter_context(nc.sbuf_tensor("S2", [128, 32768], BF16))
    S3 = es.enter_context(nc.sbuf_tensor("S3", [128, 32768], BF16))
    S4 = es.enter_context(nc.sbuf_tensor("S4", [128, 8096], BF16))
    pf = [es.enter_context(nc.psum_tensor(f"pf{i}", [128, 512], F32)) for i in range(6)]
    pb = [es.enter_context(nc.psum_tensor(f"pb{i}", [128, 1024], BF16)) for i in range(2)]

    names = ["SD", "SO", "GD", "CS", "TS", "VS", "SS", "PT", "VE", "GI"]
    sems = {n: es.enter_context(nc.semaphore(n)) for n in names}
    SD, SO, GD, CS, TS, VS, SS, PT, VE, GI = (sems[n] for n in names)
    cnt = {n: 0 for n in names}

    def bump(n, k=1):
        cnt[n] += k
        return cnt[n]

    def carve(slab, boff, p, n, dt):
        esz = 2 if dt in (BF16, U16) else 4
        c0 = boff // 2
        return slab[0:p, c0: c0 + (n * esz) // 2].bitcast(dt)

    stage = [carve(S4, 0, 128, 512, BF16), carve(S4, 1024, 128, 512, BF16)]
    bcast = carve(S4, 2048, 128, 2048, BF16)
    colq = carve(S4, 6144, 128, 512, F32)
    rowi = carve(S4, 8192, 128, 1, F32)
    ident = carve(S4, 8196, 128, 128, BF16)
    ones_col = carve(S4, 8452, 128, 1, BF16)
    mrow = carve(S4, 8456, 128, 128, F32)
    bq_sb = carve(S4, 8968, 128, NDC, F32)
    bk_sb = carve(S4, 9032, 128, NDC, F32)
    stats = carve(S4, 9096, 128, 24, F32)
    mv = carve(S4, 9192, 128, 2, F32)
    sstd = carve(S4, 9200, 128, 1, F32)
    rstd = carve(S4, 9204, 128, 1, F32)
    ones_row = carve(S4, 9208, 1, 128, F32)
    ones_rowb = carve(S4, 9720, 1, 128, BF16)
    inv_sb = carve(S4, 9976, 1, 512, F32)
    browb = carve(S4, 12024, 1, 2048, BF16)
    eps_c = carve(S4, 16120, 128, 1, F32)

    xT = [carve(S2, 0, 128, 2048, F32), carve(S2, 8192, 128, 2048, F32)]
    xn_t = [carve(S2, 16384, 128, 2048, BF16), carve(S2, 20480, 128, 2048, BF16)]

    def att_c(b):
        base = b * 17408
        return (carve(S2, base, 128, 4096, BF16),
                carve(S2, base + 8192, 128, 4096, BF16),
                carve(S2, base + 16384, 128, 512, BF16))
    att_bufs = [att_c(0), att_c(1)]
    masks = carve(S2, 34816, 128, N_MASK * 512, BF16)
    evT = [carve(S2, 47104, 128, 512, BF16), carve(S2, 48128, 128, 512, BF16)]
    ptile = [carve(S2, 49152, 128, 512, BF16), carve(S2, 50176, 128, 512, BF16),
             carve(S2, 53248, 128, 512, BF16)]
    invb = carve(S2, 51200, 128, 512, F32)
    lntmp = carve(S2, 55296, 1, 512, F32)

    p5_at = [carve(S2, 0, 128, 2048, BF16), carve(S2, 4096, 128, 2048, BF16)]
    p5_y = carve(S2, 8192, 128, 2048, F32)
    p5_x = carve(S2, 16384, 128, 2048, F32)
    p5_xn = carve(S2, 24576, 128, 2048, BF16)

    acc = carve(S2, 0, 128, 4 * 2048, F32)
    wd_ring = [carve(S2, 32768 + i * 4096, 128, 2048, BF16) for i in range(8)]

    dec_t = [carve(S1, i * 4096, 128, 1024, I32) for i in range(6)]
    pk_t = [carve(S1, 24576, 128, 1024, I32), carve(S1, 28672, 128, 1024, I32)]
    decb = [carve(S1, 32768, 128, 2048, BF16), carve(S1, 36864, 128, 2048, BF16)]
    dsc = [carve(S1, 40960, 128, 64, F32), carve(S1, 41216, 128, 64, F32)]

    wg_t = [carve(S3, 0, 128, 2048, BF16), carve(S3, 4096, 128, 2048, BF16)]
    wu_t = [carve(S3, 8192, 128, 2048, BF16), carve(S3, 12288, 128, 2048, BF16)]
    it_ring = [carve(S3, 16384 + i * 1024, 128, 512, BF16) for i in range(16)]
    g_sq = carve(S3, 32768, 128, 512, F32)
    g_t2 = carve(S3, 34816, 128, 512, F32)
    g_th = carve(S3, 36864, 128, 512, F32)
    g_b = carve(S3, 38912, 128, 512, F32)
    p6_x1 = carve(S3, 40960, 128, 2048, F32)
    p6_o = carve(S3, 49152, 128, 2048, F32)

    NPH = 7
    streams = [defaultdict(list) for _ in range(NPH)]

    def add(ph, eng, fn):
        streams[ph][eng].append(fn)

    def wait(ph, eng, sem, val):
        if val and val > 0:
            add(ph, eng, lambda e, s=sem, v=val: e.wait_ge(s, v))

    ph = 0
    add(ph, "sync", lambda e: e.dma_start(out=bq_sb, in_=bq_d).then_inc(SD, 16))
    add(ph, "sync", lambda e: e.dma_start(out=bk_sb, in_=bk_d).then_inc(SD, 16))
    add(ph, "sync", lambda e: e.dma_start(out=mrow, in_=mrow_d).then_inc(SD, 16))
    sd_p0 = bump("SD", 48)
    add(ph, "gpsimd", lambda e: e.dma_start(out=browb, in_=bv_d).then_inc(GD, 16))
    gd_bv = bump("GD", 16)
    add(ph, "gpsimd", lambda e: e.iota(colq, pattern=[[1, 512]], base=0,
                                       channel_multiplier=0,
                                       allow_small_or_imprecise_dtypes=True))
    add(ph, "gpsimd", lambda e: e.iota(rowi, pattern=[[0, 1]], base=0,
                                       channel_multiplier=1,
                                       allow_small_or_imprecise_dtypes=True).then_inc(GI, 1))
    bump("GI")
    wait(ph, "dve", GI, cnt["GI"])
    add(ph, "dve", lambda e: e.tensor_scalar(ident, colq[:, 0:128], rowi, None, ALU.is_equal))
    add(ph, "dve", lambda e: e.memset(ones_col, 1.0))
    add(ph, "dve", lambda e: e.memset(eps_c, 1e-5))
    add(ph, "dve", lambda e: e.memset(ones_rowb, 1.0))
    add(ph, "dve", lambda e: e.memset(ones_row, 1.0).then_inc(VS, 1))
    bump("VS")
    vs_ones = cnt["VS"]
    p0_cp = {}
    for i in range(4):
        wait(ph, "pe", GD, gd_bv)
        wait(ph, "pe", VS, vs_ones)
        if i >= 2:
            wait(ph, "pe", VS, p0_cp[i - 2])
        add(ph, "pe", lambda e, i=i: e.matmul(
            pf[i % 2][:, :], ones_rowb, browb[0:1, i * 512:(i + 1) * 512],
            start=True, stop=True).then_inc(TS, 1))
        bump("TS")
        wait(ph, "dve", TS, cnt["TS"])
        add(ph, "dve", lambda e, i=i: e.tensor_copy(
            bcast[:, i * 512:(i + 1) * 512], pf[i % 2][:, :]).then_inc(VS, 1))
        p0_cp[i] = bump("VS")
    pf_hist = {0: cnt["VS"], 1: cnt["VS"], 2: 0, 3: 0, 4: 0, 5: 0}

    def ln_stats(ph, src_f32):
        v0 = cnt["VS"]
        def f(e, s=src_f32, v0=v0):
            for i in range(4):
                inst = e.bn_stats(stats[:, i * 6:(i + 1) * 6], s[:, i * 512:(i + 1) * 512])
            inst.then_inc(VS, 1)
            e.wait_ge(VS, v0 + 1)
            e.bn_aggr(mv, stats).then_inc(VS, 1)
        add(ph, "dve", f)
        bump("VS", 2)
        wait(ph, "act", VS, cnt["VS"])
        add(ph, "act", lambda e: e.activation(sstd, mv[:, 1:2], AF.Sqrt,
                                              bias=eps_c).then_inc(SS, 1))
        bump("SS")
        wait(ph, "dve", SS, cnt["SS"])
        v1 = cnt["VS"]
        add(ph, "dve", lambda e, v1=v1: (e.reciprocal(rstd, sstd).then_inc(VS, 1),
                                         e.wait_ge(VS, v1 + 1)))
        bump("VS")

    pb_free = [0, 0]

    def transposes(ph, src_bf16, dst_slab, tt):
        for half in range(2):
            bi = half
            wait(ph, "pe", SS, pb_free[bi])

            def tp(e, src=src_bf16, bank=pb[bi], half=half):
                for i in range(8):
                    dc = half * 8 + i
                    inst = e.transpose(bank[:, i * 128:(i + 1) * 128],
                                       src[:, dc * 128:(dc + 1) * 128], ident)
                inst.then_inc(TS, 1)
            add(ph, "pe", tp)
            bump("TS")
            wait(ph, "act", TS, cnt["TS"])

            def ev(e, bank=pb[bi], half=half, dst=dst_slab, tt=tt):
                for i in range(8):
                    dc = half * 8 + i
                    inst = e.copy(dst[:, dc * 2048 + tt * 128: dc * 2048 + tt * 128 + 128],
                                  bank[:, i * 128:(i + 1) * 128])
                inst.then_inc(SS, 1)
            add(ph, "act", ev)
            pb_free[bi] = bump("SS")
        return cnt["TS"]

    ph = 1
    add(ph, "gpsimd", lambda e: e.dma_start(
        out=S3[:, :].rearrange("p (c h) -> p c h", c=16), in_=wk_d.rearrange("(c p) h -> p c h", p=128)).then_inc(GD, 16))
    gd_wk = bump("GD", 16)

    def ln_pass(ph, src_dram):
        xfree = [0] * 16
        src_done = [0] * 16
        for tt in range(16):
            b = tt % 2
            if tt >= 2:
                wait(ph, "sync", VS, xfree[tt - 2])
            add(ph, "sync", lambda e, b=b, tt=tt, s=src_dram: e.dma_start(
                out=xT[b], in_=s[tt * 128:(tt + 1) * 128, :]).then_inc(SD, 16))
            bump("SD", 16)
            wait(ph, "dve", SD, cnt["SD"])
            ln_stats(ph, xT[b])
            if tt >= 2:
                wait(ph, "dve", TS, src_done[tt - 2])

            v0 = cnt["VS"]
            def app(e, b=b, v0=v0):
                e.tensor_scalar(xT[b], xT[b], mv[:, 0:1], None,
                                ALU.subtract).then_inc(VS, 1)
                e.wait_ge(VS, v0 + 1)
                e.tensor_scalar(xn_t[b], xT[b], rstd, None, ALU.mult).then_inc(VS, 1)
            add(ph, "dve", app)
            bump("VS")
            xfree[tt] = bump("VS")
            wait(ph, "pe", VS, cnt["VS"])
            src_done[tt] = transposes(ph, xn_t[b], S1, tt)

    ln_pass(ph, xo)

    ph = 2
    stage_dma = [0, 0]
    stage_use = [0]

    def proj_pass(ph, wslab, bias_sb, scale, dst_ap, dst_off, tokmajor, pfb, gd_need,
                  vhead=False):
        evac = {}
        wait(ph, "dve", SD, sd_p0)
        wait(ph, "dve", GD, gd_bv)
        for a in range(16):
            for b4 in range(4):
                idx = a * 4 + b4
                bankk = pfb + idx % 2
                bank = pf[bankk]
                if idx < 2:
                    wait(ph, "pe", GD, gd_need)
                    wait(ph, "pe", VS, pf_hist[bankk])
                else:
                    wait(ph, "pe", VS, evac[idx - 2])
                if tokmajor:
                    def mm(e, a=a, b4=b4, bank=bank, ws=wslab):
                        for dc in range(NDC):
                            inst = e.matmul(
                                bank[:, :],
                                S1[:, dc * 2048 + a * 128: dc * 2048 + (a + 1) * 128],
                                ws[:, dc * 2048 + b4 * 512: dc * 2048 + (b4 + 1) * 512],
                                start=(dc == 0), stop=(dc == NDC - 1))
                        inst.then_inc(TS, 1)
                else:
                    def mm(e, a=a, b4=b4, bank=bank, ws=wslab):
                        for dc in range(NDC):
                            inst = e.matmul(
                                bank[:, :],
                                ws[:, dc * 2048 + a * 128: dc * 2048 + (a + 1) * 128],
                                S1[:, dc * 2048 + b4 * 512: dc * 2048 + (b4 + 1) * 512],
                                start=(dc == 0), stop=(dc == NDC - 1))
                        inst.then_inc(TS, 1)
                add(ph, "pe", mm)
                bump("TS")
                su = stage_use[0]
                sb = stage[su % 2]
                wait(ph, "dve", TS, cnt["TS"])
                wait(ph, "dve", SD, stage_dma[su % 2])
                if tokmajor:
                    add(ph, "dve", lambda e, sb=sb, bank=bank, b4=b4: e.tensor_tensor(
                        sb, bank[:, :], bcast[:, b4 * 512:(b4 + 1) * 512],
                        ALU.add).then_inc(VS, 1))
                else:
                    add(ph, "dve", lambda e, sb=sb, bank=bank, a=a: e.tensor_scalar(
                        sb, bank[:, :], bias_sb[:, a:a + 1], None,
                        ALU.add).then_inc(VS, 1))
                evac[idx] = pf_hist[bankk] = bump("VS")
                wait(ph, "sync", VS, cnt["VS"])
                if vhead:
                    def vst(e, sb=sb, dst=dst_ap, a=a, b4=b4, dst_off=dst_off):
                        dv = dst[:, dst_off:dst_off + 32768].rearrange(
                            "p (h a2 n) -> p h a2 n", h=16, a2=16)
                        e.dma_start(out=dv[:, b4 * 4:(b4 + 1) * 4, a, :],
                                    in_=sb.rearrange("p (h n) -> p h n", h=4)
                                    ).then_inc(SD, 16)
                    add(ph, "sync", vst)
                else:
                    off = dst_off + a * 2048 + b4 * 512
                    add(ph, "sync", lambda e, sb=sb, dst=dst_ap, off=off: e.dma_start(
                        out=dst[:, off: off + 512], in_=sb).then_inc(SD, 16))
                stage_dma[su % 2] = bump("SD", 16)
                stage_use[0] += 1

    proj_pass(ph, S3, bk_sb, None, cc2_in.ap(), 0, False, 0, gd_wk)
    k_ts = cnt["TS"]
    add(ph, "gpsimd", lambda e: e.wait_ge(TS, k_ts))
    add(ph, "gpsimd", lambda e: e.dma_start(
        out=S3[:, :].rearrange("p (c h) -> p c h", c=16),
        in_=wv_d.rearrange("(c p) h -> p c h", p=128)).then_inc(GD, 16))
    gd_wv = bump("GD", 16)
    proj_pass(ph, S3, None, None, cc2_in.ap(), 32768, True, 2, gd_wv, vhead=True)
    v_ts = cnt["TS"]
    add(ph, "gpsimd", lambda e: e.wait_ge(TS, v_ts))
    add(ph, "gpsimd", lambda e: e.dma_start(
        out=S3[:, :].rearrange("p (c h) -> p c h", c=16),
        in_=wq_d.rearrange("(c p) h -> p c h", p=128)).then_inc(GD, 16))
    gd_wq = bump("GD", 16)

    wait(ph, "sync", TS, v_ts)
    ln_pass(ph, xq_d)

    xq_sd = cnt["SD"]
    wait(ph, "gpsimd", SD, xq_sd)
    wait(ph, "gpsimd", GD, gd_wq)
    add(ph, "gpsimd", lambda e: e.collective_compute(
        "AllGather", ALU.bypass,
        replica_groups=[[0, 1], [2, 3], [4, 5], [6, 7]],
        ins=[cc2_in.ap().opt()], outs=[cc2_out.ap().opt()]).then_inc(CS, 1))
    cs_kv = bump("CS")

    S2q = S2[:, :].rearrange("p (h bl rr a) -> p h bl rr a", h=16, bl=2, rr=64, a=16)
    q_evac = {}
    for a in range(16):
        for b4 in range(4):
            idx = a * 4 + b4
            bankk = idx % 2
            bank = pf[bankk]
            if idx < 2:
                wait(ph, "pe", GD, gd_wq)
                wait(ph, "pe", SS, cnt["SS"])
                wait(ph, "pe", VS, pf_hist[bankk])
            else:
                wait(ph, "pe", VS, q_evac[idx - 2])

            def mmq(e, a=a, b4=b4, bank=bank):
                for dc in range(NDC):
                    inst = e.matmul(
                        bank[:, :],
                        S3[:, dc * 2048 + a * 128: dc * 2048 + (a + 1) * 128],
                        S1[:, dc * 2048 + b4 * 512: dc * 2048 + (b4 + 1) * 512],
                        start=(dc == 0), stop=(dc == NDC - 1))
                inst.then_inc(TS, 1)
            add(ph, "pe", mmq)
            bump("TS")
            wait(ph, "dve", TS, cnt["TS"])

            def qev(e, a=a, b4=b4, bank=bank):
                dst = S2q[:, b4 * 4:(b4 + 1) * 4, :, :, a]
                srcv = bank[:, :].rearrange("p (h bl rr) -> p h bl rr", h=4, bl=2)
                e.tensor_scalar(dst, srcv, bq_sb[:, a:a + 1], INV_SQRT_DH,
                                ALU.add, ALU.mult).then_inc(VS, 1)
            add(ph, "dve", qev)
            q_evac[idx] = pf_hist[bankk] = bump("VS")
    wait(ph, "sync", VS, cnt["VS"])
    wait(ph, "sync", CS, cs_kv)
    add(ph, "sync", lambda e: e.dma_start(out=qT_d, in_=S2[:, :]).then_inc(SD, 16))
    bump("SD", 16)
    p2_sd = cnt["SD"]

    ph = 3

    dec_used, dec_tp, flush_sd = {}, {}, {}
    for mi, (wp_d, ws_d) in enumerate([(gwp, gws), (uwp, uws), (dwp, dws)]):
        for g in range(16):
            b = g % 2
            prev_used = dec_used.get((mi, g - 2)) or (dec_used.get((mi - 1, 14 + g)) if mi > 0 else None)
            if prev_used:
                wait(ph, "sync", VS, prev_used)
            add(ph, "sync", lambda e, b=b, g=g, wp=wp_d: e.dma_start(
                out=pk_t[b], in_=wp[g * 128:(g + 1) * 128, :]).then_inc(SD, 16))
            add(ph, "sync", lambda e, b=b, g=g, ws=ws_d: e.dma_start(
                out=dsc[b], in_=ws[g * 128:(g + 1) * 128, :]).then_inc(SD, 16))
            bump("SD", 32)
            wait(ph, "dve", SD, cnt["SD"])
            prev_tp = dec_tp.get((mi, g - 2)) or (dec_tp.get((mi - 1, 14 + g)) if mi > 0 else None)
            if prev_tp:
                wait(ph, "dve", TS, prev_tp)

            v0 = cnt["VS"]
            def dec(e, b=b, v0=v0):
                t0, t1, t2, t3, t4, t5 = dec_t
                w = pk_t[b]
                v = v0
                def st(inst):
                    nonlocal v
                    v += 1
                    inst.then_inc(VS, 1)
                    e.wait_ge(VS, v)
                st(e.tensor_scalar(t0, w, 7, None, ALU.bitwise_and))
                st(e.tensor_scalar(t1, w, 4, 7, ALU.logical_shift_right, ALU.bitwise_and))
                st(e.tensor_scalar(t2, t0, 64, 16128, ALU.mult, ALU.add))
                st(e.scalar_tensor_tensor(t3, t0, 16128.0, t2, ALU.mult, ALU.min))
                st(e.tensor_scalar(t4, t1, 64, 16128, ALU.mult, ALU.add))
                st(e.scalar_tensor_tensor(t5, t1, 16128.0, t4, ALU.mult, ALU.min))
                st(e.tensor_scalar(t0, w, 8, 12, ALU.bitwise_and, ALU.logical_shift_left))
                st(e.tensor_scalar(t1, w, 128, 8, ALU.bitwise_and, ALU.logical_shift_left))
                st(e.tensor_tensor(t2, t3, t0, ALU.add))
                st(e.tensor_tensor(t4, t5, t1, ALU.add))
                dvv = decb[b].rearrange("p (a c) -> p a c", c=32)
                t2b = t2.bitcast(BF16).rearrange("p (a c) -> p a c", c=32)
                t4b = t4.bitcast(BF16).rearrange("p (a c) -> p a c", c=32)
                scb = dsc[b].unsqueeze(2).broadcast_to([128, 64, 16])
                st(e.scalar_tensor_tensor(dvv[:, :, 0:32:2], t2b[:, :, 0:32:2], 1.0,
                                          scb, ALU.mult, ALU.mult))
                st(e.scalar_tensor_tensor(dvv[:, :, 1:32:2], t4b[:, :, 0:32:2], 1.0,
                                          scb, ALU.mult, ALU.mult))
            add(ph, "dve", dec)
            dec_used[(mi, g)] = bump("VS", 12)
            vs_dec = cnt["VS"]
            for half in range(2):
                bi = half
                wait(ph, "pe", VS, vs_dec)
                wait(ph, "pe", SS, pb_free[bi])
                if mi > 0 and g == 0 and half == 0:
                    wait(ph, "pe", SD, flush_sd[mi - 1])

                def tp(e, b=b, bank=pb[bi], half=half):
                    for i in range(8):
                        dc = half * 8 + i
                        inst = e.transpose(bank[:, i * 128:(i + 1) * 128],
                                           decb[b][:, dc * 128:(dc + 1) * 128], ident)
                    inst.then_inc(TS, 1)
                add(ph, "pe", tp)
                bump("TS")
                wait(ph, "act", TS, cnt["TS"])

                def ev(e, bank=pb[bi], half=half, g=g):
                    for i in range(8):
                        dc = half * 8 + i
                        inst = e.copy(S3[:, dc * 2048 + g * 128: dc * 2048 + g * 128 + 128],
                                      bank[:, i * 128:(i + 1) * 128])
                    inst.then_inc(SS, 1)
                add(ph, "act", ev)
                pb_free[bi] = bump("SS")
            dec_tp[(mi, g)] = cnt["TS"]
        wait(ph, "sync", SS, cnt["SS"])
        add(ph, "sync", lambda e, mi=mi: e.dma_start(
            out=cc1_in[mi].ap(), in_=S3[:, :]).then_inc(SD, 16))
        flush_sd[mi] = bump("SD", 16)
    wait(ph, "gpsimd", SD, flush_sd[2])
    add(ph, "gpsimd", lambda e: e.dma_start(
        out=S3[:, :].rearrange("p (c h) -> p c h", c=16), in_=wo_d.rearrange("(c p) h -> p c h", p=128)).then_inc(GD, 16))
    gd_wo = bump("GD", 16)
    p3_sd = cnt["SD"]

    ph = 4
    for eng in ("pe", "act", "dve", "sync"):
        wait(ph, eng, CS, cs_kv)
    wait(ph, "sync", SD, p3_sd)
    wait(ph, "gpsimd", CS, cs_kv)
    wait(ph, "gpsimd", SD, flush_sd[2])
    for mi in range(3):
        add(ph, "gpsimd", lambda e, mi=mi: e.collective_compute(
            "AllGather", ALU.bypass, replica_groups=[list(range(R))],
            ins=[cc1_in[mi].ap().opt()], outs=[cc1_out[mi].ap().opt()]).then_inc(CS, 1))
        bump("CS")
    cs_w = cnt["CS"]

    att_done, att_evac, att_binv, att_edma, att_recip = {}, {}, {}, {}, {}

    def prev_key(s, h, lag):
        if h - lag >= 0:
            return (s, h - lag)
        if s > 0:
            return (s - 1, 16 + h - lag)
        return None

    for s in range(4):
        lt = SLOT_LT[s]
        kmax = SLOT_KMAX[s]
        mstart = SLOT_MSTART[s]
        runs = kv_runs(kmax)

        def gen_masks(e, s=s, kmax=kmax, mstart=mstart):
            for jm in range(mstart, kmax):
                inst = e.tensor_scalar(
                    masks[:, (jm - mstart) * 512:(jm - mstart + 1) * 512],
                    colq, mrow[:, s * 32 + jm: s * 32 + jm + 1], None, ALU.is_ge)
            inst.then_inc(VS, 1)
        add(ph, "dve", gen_masks)
        mask_vs = bump("VS")

        for h in range(NH):
            bi = h % 2
            kc, vc, qc = att_bufs[bi]
            pk2 = prev_key(s, h, 2)
            pk1 = prev_key(s, h, 1)
            if pk2 and pk2 in att_done:
                wait(ph, "sync", TS, att_done[pk2])
                wait(ph, "act", TS, att_done[pk2])
            for (j0, r, tc0, ln) in runs:
                add(ph, "sync", lambda e, j0=j0, r=r, tc0=tc0, ln=ln, kc=kc, h=h: e.dma_start(
                    out=kc[:, j0 * 128:(j0 + ln) * 128],
                    in_=cc2_out.ap()[r * 128:(r + 1) * 128,
                                     h * 2048 + tc0 * 128: h * 2048 + (tc0 + ln) * 128]
                ).then_inc(SD, 16))
                bump("SD", 16)

                def vload(e, j0=j0, r=r, tc0=tc0, ln=ln, vc=vc, h=h):
                    e.dma_start(
                        out=vc[:, j0 * 128:(j0 + ln) * 128],
                        in_=cc2_out.ap()[r * 128:(r + 1) * 128,
                                         32768 + h * 2048 + tc0 * 128:
                                         32768 + h * 2048 + (tc0 + ln) * 128]
                    ).then_inc(GD, 16)
                add(ph, "act", vload)
                bump("GD", 16)
            dma_tgt = cnt["SD"]
            add(ph, "act", lambda e, qc=qc, h=h, lt=lt: e.dma_start(
                out=qc, in_=qT_d[:, h * 2048 + lt * 512: h * 2048 + lt * 512 + 512]
            ).then_inc(GD, 16))
            dma_tgt_gd = bump("GD", 16)

            av_bank = pf[3 + bi]
            sm_bank = pf[5]
            exp_ss = {}
            av_ts = {}
            pt_of = {}

            def emit_st(j):
                sbank = pf[j % 3]
                if j == 0:
                    wait(ph, "pe", SD, dma_tgt)
                    wait(ph, "pe", GD, dma_tgt_gd)
                    wait(ph, "pe", VS, mask_vs)
                    pe_evac = att_evac.get(pk2) if pk2 else None
                    if pe_evac:
                        wait(ph, "pe", VE, pe_evac)
                    if pk1 and pk1 in att_recip:
                        wait(ph, "pe", SS, att_recip[pk1])
                if j == 2 and pk1 and pk1 in att_binv:
                    wait(ph, "pe", VE, att_binv[pk1])
                if j >= 3:
                    wait(ph, "pe", SS, exp_ss[j - 3])
                add(ph, "pe", lambda e, sbank=sbank, kc=kc, qc=qc, j=j: e.matmul(
                    sbank[:, :], kc[:, j * 128:(j + 1) * 128], qc,
                    start=True, stop=True).then_inc(TS, 1))
                st_ts = bump("TS")
                wait(ph, "act", TS, st_ts)
                if j >= 3:
                    wait(ph, "act", TS, av_ts[j - 3])
                elif pk1 and pk1 in att_done:
                    wait(ph, "act", TS, att_done[pk1])
                masked = j >= mstart
                add(ph, "act", lambda e, j=j, sbank=sbank: e.activation(
                    ptile[j % 3], sbank[:, :], AF.Exp).then_inc(SS, 1))
                exp_ss[j] = bump("SS")
                if masked:
                    wait(ph, "dve", SS, exp_ss[j])
                    wait(ph, "dve", VS, mask_vs)
                    add(ph, "dve", lambda e, j=j, mstart=mstart: e.tensor_tensor(
                        ptile[j % 3], ptile[j % 3],
                        masks[:, (j - mstart) * 512:(j - mstart + 1) * 512],
                        ALU.mult).then_inc(PT, 1))
                    pt_of[j] = ("PT", bump("PT"))
                else:
                    pt_of[j] = ("SS", exp_ss[j])

            def emit_av(j):
                kind, val = pt_of[j]
                wait(ph, "pe", PT if kind == "PT" else SS, val)

                def av(e, j=j, vc=vc, av_bank=av_bank, sm_bank=sm_bank, kmax=kmax):
                    e.matmul(av_bank[:, :], vc[:, j * 128:(j + 1) * 128], ptile[j % 3],
                             start=(j == 0), stop=(j == kmax - 1))
                    e.matmul(sm_bank[0:1, :], ones_col, ptile[j % 3],
                             start=(j == 0), stop=(j == kmax - 1)).then_inc(TS, 1)
                add(ph, "pe", av)
                av_ts[j] = bump("TS")

            for j in range(kmax):
                emit_st(j)
                if j >= 2:
                    emit_av(j - 2)
            emit_av(kmax - 2)
            emit_av(kmax - 1)
            att_done[(s, h)] = cnt["TS"]
            wait(ph, "act", TS, cnt["TS"])
            s0 = cnt["SS"]
            def lninv(e, sm_bank=sm_bank, s0=s0):
                e.activation(lntmp, sm_bank[0:1, :], AF.Ln).then_inc(SS, 1)
                e.wait_ge(SS, s0 + 1)
                e.activation(inv_sb, lntmp, AF.Exp, scale=-1.0).then_inc(SS, 1)
            add(ph, "act", lninv)
            bump("SS")
            att_recip[(s, h)] = bump("SS")
            wait(ph, "pe", SS, cnt["SS"])
            add(ph, "pe", lambda e: e.matmul(pf[2][:, :], ones_row, inv_sb,
                                             start=True, stop=True).then_inc(TS, 1))
            bump("TS")
            wait(ph, "dve", TS, cnt["TS"])
            add(ph, "dve", lambda e: e.tensor_copy(invb, pf[2][:, :]).then_inc(VE, 1))
            att_binv[(s, h)] = bump("VE")
            wait(ph, "dve", VE, att_binv[(s, h)])
            if pk2 and pk2 in att_edma:
                wait(ph, "dve", SD, att_edma[pk2])
            add(ph, "dve", lambda e, bi=bi, av_bank=av_bank: e.tensor_tensor(
                evT[bi], av_bank[:, :], invb, ALU.mult).then_inc(VE, 1))
            att_evac[(s, h)] = bump("VE")
            wait(ph, "sync", VE, cnt["VE"])
            add(ph, "sync", lambda e, bi=bi, h=h, lt=lt: e.dma_start(
                out=attnT_d[:, h * 2048 + lt * 512: h * 2048 + lt * 512 + 512],
                in_=evT[bi]).then_inc(SD, 16))
            att_edma[(s, h)] = bump("SD", 16)
    p4_sd = cnt["SD"]

    ph = 5
    wait(ph, "sync", SD, p4_sd)
    add(ph, "gpsimd", lambda e: e.dma_start(out=browb, in_=bo_d).then_inc(GD, 16))
    gd_bo = bump("GD", 16)
    p5_cp = {}
    for i in range(4):
        wait(ph, "pe", GD, gd_bo)
        wait(ph, "pe", GD, gd_wo)
        if i >= 2:
            wait(ph, "pe", VS, p5_cp[i - 2])
        else:
            wait(ph, "pe", VS, pf_hist[i])
        add(ph, "pe", lambda e, i=i: e.matmul(
            pf[i % 2][:, :], ones_rowb, browb[0:1, i * 512:(i + 1) * 512],
            start=True, stop=True).then_inc(TS, 1))
        bump("TS")
        wait(ph, "dve", TS, cnt["TS"])
        add(ph, "dve", lambda e, i=i: e.tensor_copy(
            bcast[:, i * 512:(i + 1) * 512], pf[i % 2][:, :]).then_inc(VS, 1))
        p5_cp[i] = pf_hist[i] = bump("VS")
    bo_vs = cnt["VS"]

    p5_mm, p5_ev, p5_ln3, p5_x1sd, p5_tp = {}, {}, {}, {}, {}
    for tt in range(16):
        b = tt % 2
        at = p5_at[b]
        if tt >= 2:
            wait(ph, "sync", TS, p5_mm[tt - 2])
        if tt >= 1:
            wait(ph, "sync", VS, p5_ln3[tt - 1])
            wait(ph, "sync", SD, p5_x1sd[tt - 1])

        def at_load(e, at=at, tt=tt):
            src = attnT_d.rearrange("p (h t) -> p h t", h=NH)[:, :, tt * 128:(tt + 1) * 128]
            e.dma_start(out=at.rearrange("p (h n) -> p h n", h=NH), in_=src).then_inc(SD, 16)
        add(ph, "sync", at_load)
        add(ph, "sync", lambda e, tt=tt: e.dma_start(
            out=p5_x, in_=xo[tt * 128:(tt + 1) * 128, :]).then_inc(SD, 16))
        x_sd = bump("SD", 32)
        for dt in range(4):
            idx = tt * 4 + dt
            bankk = 2 + idx % 2
            bank = pf[bankk]
            if idx == 0:
                wait(ph, "pe", VS, bo_vs)
            wait(ph, "pe", SD, x_sd)
            if idx >= 2:
                wait(ph, "pe", VS, p5_ev[idx - 2])
            else:
                wait(ph, "pe", VS, pf_hist[bankk])

            def mm(e, at=at, bank=bank, dt=dt):
                for hc in range(NH):
                    inst = e.matmul(bank[:, :], at[:, hc * 128:(hc + 1) * 128],
                                    S3[:, hc * 2048 + dt * 512: hc * 2048 + (dt + 1) * 512],
                                    start=(hc == 0), stop=(hc == NH - 1))
                inst.then_inc(TS, 1)
            add(ph, "pe", mm)
            bump("TS")
            wait(ph, "dve", TS, cnt["TS"])
            if dt == 0 and tt >= 1:
                wait(ph, "dve", VS, p5_ln3[tt - 1])
            add(ph, "dve", lambda e, bank=bank, dt=dt: e.tensor_tensor(
                p5_y[:, dt * 512:(dt + 1) * 512], bank[:, :],
                bcast[:, dt * 512:(dt + 1) * 512], ALU.add).then_inc(VS, 1))
            p5_ev[idx] = pf_hist[bankk] = bump("VS")
        p5_mm[tt] = cnt["TS"]
        wait(ph, "dve", VS, p5_ev[tt * 4 + 3])
        ln_stats(ph, p5_y)

        v0 = cnt["VS"]
        def ln2_apply(e, v0=v0):
            e.tensor_scalar(p5_y, p5_y, mv[:, 0:1], None, ALU.subtract).then_inc(VS, 1)
            e.wait_ge(VS, v0 + 1)
            e.tensor_scalar(p5_y, p5_y, rstd, None, ALU.mult).then_inc(VS, 1)
            e.wait_ge(VS, v0 + 2)
            e.tensor_tensor(p5_x, p5_x, p5_y, ALU.add).then_inc(VS, 1)
        add(ph, "dve", ln2_apply)
        bump("VS", 3)
        wait(ph, "sync", VS, cnt["VS"])
        add(ph, "sync", lambda e, tt=tt: e.dma_start(
            out=x1_d[tt * 128:(tt + 1) * 128, :], in_=p5_x).then_inc(SD, 16))
        p5_x1sd[tt] = bump("SD", 16)
        ln_stats(ph, p5_x)
        if tt >= 1:
            wait(ph, "dve", TS, p5_tp[tt - 1])

        v0 = cnt["VS"]
        def ln3_apply(e, v0=v0):
            e.tensor_scalar(p5_y, p5_x, mv[:, 0:1], None, ALU.subtract).then_inc(VS, 1)
            e.wait_ge(VS, v0 + 1)
            e.tensor_scalar(p5_xn, p5_y, rstd, None, ALU.mult).then_inc(VS, 1)
        add(ph, "dve", ln3_apply)
        bump("VS")
        p5_ln3[tt] = bump("VS")
        wait(ph, "pe", VS, cnt["VS"])
        p5_tp[tt] = transposes(ph, p5_xn, S1, tt)
    p5_sd = cnt["SD"]

    ph = 6
    wait(ph, "sync", CS, cs_w)
    wait(ph, "sync", SD, p5_sd)

    gu_ts = {}
    inter_vs = {}
    t3_vs = {}
    wdma = {}
    down_ts = {}
    down_hist = {4: pf_hist[4], 5: pf_hist[5]}
    flush_slice = {}
    out_add_vs = {}
    out_so = {}
    n_out = 0

    def emit_down_group(to, g8):
        dg = to * 16 + g8
        last_gidx = to * 128 + g8 * 8 + 7
        for ts4 in range(4):
            for ht in range(4):
                q = ts4 * 4 + ht
                bankk = 4 + q % 2
                bank = pf[bankk]
                if q == 0:
                    wait(ph, "pe", VS, inter_vs[last_gidx])
                    wait(ph, "pe", SD, wd_sd[dg])
                wait(ph, "pe", VS, down_hist[bankk])

                def dmm(e, ts4=ts4, ht=ht, bank=bank, g8=g8):
                    for i8 in range(8):
                        sl = (g8 * 8 + i8) % 16
                        inst = e.matmul(
                            bank[:, :],
                            it_ring[sl][:, ts4 * 128:(ts4 + 1) * 128],
                            wd_ring[i8][:, ht * 512:(ht + 1) * 512],
                            start=(i8 == 0), stop=(i8 == 7))
                    inst.then_inc(TS, 1)
                add(ph, "pe", dmm)
                bump("TS")
                wait(ph, "dve", TS, cnt["TS"])
                pfl = flush_slice.get((ts4, ht))
                if pfl:
                    wait(ph, "dve", VS, pfl)
                if g8 == 0:
                    add(ph, "dve", lambda e, bank=bank, ts4=ts4, ht=ht: e.tensor_copy(
                        acc[:, ts4 * 2048 + ht * 512: ts4 * 2048 + (ht + 1) * 512],
                        bank[:, :]).then_inc(VS, 1))
                else:
                    add(ph, "dve", lambda e, bank=bank, ts4=ts4, ht=ht: e.tensor_tensor(
                        acc[:, ts4 * 2048 + ht * 512: ts4 * 2048 + (ht + 1) * 512],
                        acc[:, ts4 * 2048 + ht * 512: ts4 * 2048 + (ht + 1) * 512],
                        bank[:, :], ALU.add).then_inc(VS, 1))
                down_hist[bankk] = flush_slice[(ts4, ht)] = bump("VS")
        down_ts[dg] = cnt["TS"]

    def emit_wd_load(to, c):
        g_of_c = c // 8
        dgp = to * 16 + g_of_c - 1
        if dgp >= 0 and dgp in down_ts:
            wait(ph, "sync", TS, down_ts[dgp])
        r, o = c // 16, c % 16
        add(ph, "sync", lambda e, r=r, o=o, ring_i=c % 8: e.dma_start(
            out=wd_ring[ring_i],
            in_=cc1_out[2].ap()[r * 128:(r + 1) * 128,
                                o * 2048:(o + 1) * 2048]).then_inc(SD, 16))
        wd_sd[to * 16 + g_of_c] = bump("SD", 16)

    wd_sd = {}
    for to in range(4):
        for nci in range(128):
            r, o = nci // 16, nci % 16
            buf = nci % 2
            ring_i = nci % 16
            g8 = nci // 8
            gidx = to * 128 + nci
            if gidx >= 2:
                wait(ph, "sync", TS, gu_ts[gidx - 2])

            def wload(e, r=r, o=o, buf=buf):
                src_g = cc1_out[0].ap()[r * 128:(r + 1) * 128, :]
                src_g = src_g.rearrange("p (c n) -> p c n", c=16)[:, :, o * 128:(o + 1) * 128]
                e.dma_start(out=wg_t[buf].rearrange("p (c n) -> p c n", c=16),
                            in_=src_g).then_inc(SD, 16)
                src_u = cc1_out[1].ap()[r * 128:(r + 1) * 128, :]
                src_u = src_u.rearrange("p (c n) -> p c n", c=16)[:, :, o * 128:(o + 1) * 128]
                e.dma_start(out=wu_t[buf].rearrange("p (c n) -> p c n", c=16),
                            in_=src_u).then_inc(SD, 16)
            add(ph, "sync", wload)
            wdma[gidx] = bump("SD", 32)
            if nci >= 8:
                emit_wd_load(to, nci - 8)

            wait(ph, "pe", SD, wdma[gidx])
            if gidx >= 2:
                wait(ph, "pe", VS, inter_vs[gidx - 2])
            else:
                wait(ph, "pe", VS, max(pf_hist[buf], pf_hist[2 + buf]))

            def gmm(e, buf=buf, to=to):
                for c in range(NDC):
                    inst = e.matmul(pf[buf][:, :], wg_t[buf][:, c * 128:(c + 1) * 128],
                                    S1[:, c * 2048 + to * 512: c * 2048 + (to + 1) * 512],
                                    start=(c == 0), stop=(c == NDC - 1))
                inst.then_inc(TS, 1)
            add(ph, "pe", gmm)
            g_ts = bump("TS")

            def umm(e, buf=buf, to=to):
                for c in range(NDC):
                    inst = e.matmul(pf[2 + buf][:, :], wu_t[buf][:, c * 128:(c + 1) * 128],
                                    S1[:, c * 2048 + to * 512: c * 2048 + (to + 1) * 512],
                                    start=(c == 0), stop=(c == NDC - 1))
                inst.then_inc(TS, 1)
            add(ph, "pe", umm)
            gu_ts[gidx] = bump("TS")

            if nci % 8 == 7 and g8 >= 1:
                emit_down_group(to, g8 - 1)

            wait(ph, "act", TS, g_ts)
            if gidx >= 1:
                wait(ph, "act", VS, inter_vs[gidx - 1])
            add(ph, "act", lambda e, buf=buf: e.activation(
                g_sq, pf[buf][:, :], AF.Square).then_inc(SS, 1))
            sq_ss = bump("SS")
            wait(ph, "dve", SS, sq_ss)
            v0 = cnt["VS"]
            add(ph, "dve", lambda e, v0=v0: (
                e.tensor_scalar(g_t2, g_sq, 0.044715 * 0.7978845608028654,
                                0.7978845608028654, ALU.mult, ALU.add).then_inc(VS, 1),
                e.wait_ge(VS, v0 + 1)))
            bump("VS")
            add(ph, "dve", lambda e, buf=buf: e.tensor_tensor(
                g_t2, g_t2, pf[buf][:, :], ALU.mult).then_inc(VS, 1))
            t3_vs[gidx] = bump("VS")
            wait(ph, "act", VS, t3_vs[gidx])
            add(ph, "act", lambda e: e.activation(
                g_th, g_t2, AF.Tanh).then_inc(SS, 1))
            th_ss = bump("SS")
            wait(ph, "dve", SS, th_ss)
            v0 = cnt["VS"]
            add(ph, "dve", lambda e, buf=buf, v0=v0: (
                e.scalar_tensor_tensor(g_b, g_th, 1.0, pf[buf][:, :],
                                       ALU.add, ALU.mult).then_inc(VS, 1),
                e.wait_ge(VS, v0 + 1)))
            bump("VS")
            wait(ph, "dve", TS, gu_ts[gidx])
            add(ph, "dve", lambda e, buf=buf, ring_i=ring_i: e.scalar_tensor_tensor(
                it_ring[ring_i], g_b, 0.5, pf[2 + buf][:, :],
                ALU.mult, ALU.mult).then_inc(VS, 1))
            inter_vs[gidx] = pf_hist[buf] = pf_hist[2 + buf] = bump("VS")

        for c in range(120, 128):
            emit_wd_load(to, c)
        emit_down_group(to, 15)

        for ts4 in range(4):
            acc_sl = acc[:, ts4 * 2048:(ts4 + 1) * 2048]
            pk = out_add_vs.get((to - 1, ts4))
            if pk:
                wait(ph, "gpsimd", VS, pk)
            add(ph, "gpsimd", lambda e, to=to, ts4=ts4: e.dma_start(
                out=p6_x1, in_=x1_d[to * 512 + ts4 * 128: to * 512 + (ts4 + 1) * 128, :]
            ).then_inc(GD, 16))
            x1_sd = bump("GD", 16)
            wait(ph, "dve", VS, flush_slice[(ts4, 3)])
            ln_stats(ph, acc_sl)

            v0 = cnt["VS"]
            def ln4_apply(e, acc_sl=acc_sl, v0=v0):
                e.tensor_scalar(acc_sl, acc_sl, mv[:, 0:1], None,
                                ALU.subtract).then_inc(VS, 1)
                e.wait_ge(VS, v0 + 1)
                e.tensor_scalar(acc_sl, acc_sl, rstd, None, ALU.mult).then_inc(VS, 1)
            add(ph, "dve", ln4_apply)
            bump("VS", 2)
            wait(ph, "dve", GD, x1_sd)
            if n_out >= 1:
                wait(ph, "dve", SO, out_so[n_out - 1])
            add(ph, "dve", lambda e, acc_sl=acc_sl: e.tensor_tensor(
                p6_o, acc_sl, p6_x1, ALU.add).then_inc(VS, 1))
            out_add_vs[(to, ts4)] = bump("VS")
            wait(ph, "gpsimd", VS, cnt["VS"])
            add(ph, "gpsimd", lambda e, to=to, ts4=ts4: e.dma_start(
                out=out_d[to * 512 + ts4 * 128: to * 512 + (ts4 + 1) * 128, :],
                in_=p6_o).then_inc(SO, 16))
            out_so[n_out] = bump("SO", 16)
            n_out += 1

    wait(ph, "sync", SO, cnt["SO"])
    if dbg:
        add(ph, "sync", lambda e: e.dma_start(out=d_qT, in_=qT_d).then_inc(SO, 16))
        add(ph, "sync", lambda e: e.dma_start(out=d_kv, in_=cc2_out.ap()).then_inc(SO, 16))
        add(ph, "sync", lambda e: e.dma_start(out=d_att, in_=attnT_d).then_inc(SO, 16))
        add(ph, "sync", lambda e: e.dma_start(out=d_x1, in_=x1_d).then_inc(SO, 16))
        add(ph, "sync", lambda e: e.dma_start(
            out=d_w, in_=cc1_out.ap()[:, 0:2048]).then_inc(SO, 16))
        add(ph, "sync", lambda e: e.dma_start(
            out=d_wd, in_=cc1_out.ap()[:, 65536:67584]).then_inc(SO, 16))
        bump("SO", 96)
        wait(ph, "sync", SO, cnt["SO"])

    def runner(lst):
        return lambda e: [f(e) for f in lst]

    with es:
        for p in range(NPH):
            with nc.Block() as blk:
                blk.tensor(runner(streams[p]["pe"]))
                blk.vector(runner(streams[p]["dve"]))
                blk.scalar(runner(streams[p]["act"]))
                blk.gpsimd(runner(streams[p]["gpsimd"]))
                blk.sync(runner(streams[p]["sync"]))
    return nc


_NC_CACHE = [None]
DBG = [False]
_LAST_RES = [None]


def kernel(L0_attn_x, L0_attn_Wq, L0_attn_bq, L0_attn_Wk, L0_attn_bk,
           L0_attn_Wv, L0_attn_bv, L0_attn_Wo, L0_attn_bo,
           L0_ffn_gate_W_packed, L0_ffn_gate_scales,
           L0_ffn_up_W_packed, L0_ffn_up_scales,
           L0_ffn_down_W_packed, L0_ffn_down_scales):
    x = np.ascontiguousarray(np.asarray(L0_attn_x, dtype=np.float32))
    wq = np.ascontiguousarray(np.asarray(L0_attn_Wq, dtype=np.float32))
    wk = np.ascontiguousarray(np.asarray(L0_attn_Wk, dtype=np.float32))
    wv = np.ascontiguousarray(np.asarray(L0_attn_Wv, dtype=np.float32))
    wo = np.ascontiguousarray(np.asarray(L0_attn_Wo, dtype=np.float32))
    bq = np.asarray(L0_attn_bq, dtype=np.float32)
    bk = np.asarray(L0_attn_bk, dtype=np.float32)
    bv = np.asarray(L0_attn_bv, dtype=np.float32)
    bo = np.asarray(L0_attn_bo, dtype=np.float32)
    gp = np.asarray(L0_ffn_gate_W_packed, dtype=np.int32)
    gs = np.asarray(L0_ffn_gate_scales, dtype=np.float32)
    up = np.asarray(L0_ffn_up_W_packed, dtype=np.int32)
    us = np.asarray(L0_ffn_up_scales, dtype=np.float32)
    dp = np.asarray(L0_ffn_down_W_packed, dtype=np.int32)
    ds = np.asarray(L0_ffn_down_scales, dtype=np.float32)

    bq_r = np.ascontiguousarray(bq.reshape(NDC, 128).T)
    bk_r = np.ascontiguousarray(bk.reshape(NDC, 128).T)
    bv_row = np.ascontiguousarray(bv.reshape(1, D))
    bo_row = np.ascontiguousarray(bo.reshape(1, D))

    in_maps = []
    for c in range(R):
        b = c // 2
        half = c % 2
        blocks = [0, 3] if half == 0 else [1, 2]
        x_own = np.concatenate([x[b, blk * 1024:(blk + 1) * 1024, :] for blk in blocks],
                               axis=0)
        qrows = []
        for h in range(NH):
            for blk in blocks:
                qrows.extend(range(h * 256 + blk * 64, h * 256 + blk * 64 + 64))
        xq_own = x[b][np.asarray(qrows)]
        maskrow = np.zeros((128, 128), dtype=np.float32)
        si = np.arange(128, dtype=np.float32)[:, None]
        for s in range(4):
            lt = SLOT_LT[s]
            qg0 = QG0[half][lt]
            for j in range(32):
                maskrow[:, s * 32 + j: s * 32 + j + 1] = si + (j * 128 - qg0)
        in_maps.append({
            "xo": np.ascontiguousarray(x_own),
            "xq": np.ascontiguousarray(xq_own),
            "wq": wq, "wk": wk, "wv": wv, "wo": wo,
            "bq_r": bq_r, "bk_r": bk_r, "bv_row": bv_row, "bo_row": bo_row,
            "gwp": np.ascontiguousarray(gp[c * 2048:(c + 1) * 2048, :]),
            "gws": np.ascontiguousarray(gs[c * 2048:(c + 1) * 2048, :]),
            "uwp": np.ascontiguousarray(up[c * 2048:(c + 1) * 2048, :]),
            "uws": np.ascontiguousarray(us[c * 2048:(c + 1) * 2048, :]),
            "dwp": np.ascontiguousarray(dp[:, c * 1024:(c + 1) * 1024]),
            "dws": np.ascontiguousarray(ds[:, c * 64:(c + 1) * 64]),
            "maskrow": maskrow,
        })

    if _NC_CACHE[0] is None:
        _NC_CACHE[0] = build_program(dbg=DBG[0])
    nc = _NC_CACHE[0]
    res = run_bass_kernel_spmd(nc, in_maps, core_ids=list(range(R)))
    if DBG[0]:
        _LAST_RES[0] = res

    out = np.empty((B_, T_, D), dtype=np.float32)
    for c in range(R):
        b = c // 2
        half = c % 2
        blocks = [0, 3] if half == 0 else [1, 2]
        o = np.asarray(res.results[c]["out"], dtype=np.float32)
        for i, blk in enumerate(blocks):
            out[b, blk * 1024:(blk + 1) * 1024, :] = o[i * 1024:(i + 1) * 1024, :]
    return out
